# version 32
# baseline (speedup 1.0000x reference)
import numpy as np
import ml_dtypes
from contextlib import ExitStack

import concourse.bass as bass
import concourse.mybir as mybir
from concourse.bass_utils import run_bass_kernel_spmd

F32 = mybir.dt.float32
BF16 = mybir.dt.bfloat16
I32 = mybir.dt.int32
OP = mybir.AluOpType
AF = mybir.ActivationFunctionType
AX = mybir.AxisListType

NCORES = 8
B, L, DM = 8, 2048, 2048
H, N, HD = 32, 128, 64
MLP = 16384
T = 512
NPASS = L // T
KC = DM // 128
NT = T // 128
SLICE_N = MLP // NCORES
SLICE_J = SLICE_N // 2
GELU_C0 = 0.7978845608028654
GELU_C1 = 0.044715 * GELU_C0
CC_ROWS = 3072
CC_COLS = 4096
ENGINES = ("sync", "tensor", "vector", "scalar", "gpsimd")


def build_perm():
    perm = np.empty(DM, dtype=np.int64)
    for c in range(KC):
        J, par = c // 2, c % 2
        perm[c * 128:(c + 1) * 128] = 256 * J + 2 * np.arange(128) + par
    return perm


class Ctx:

    def __init__(self, sem_handles):
        self.sem_handles = sem_handles

    def begin(self, name, eng):
        self.active = name
        self.eng = eng
        self.count = {k: 0 for k in self.sem_handles}

    def inc(self, instr, semname, k=1):
        if instr is not None:
            instr.then_inc(self.sem_handles[semname], k)
        self.count[semname] += k
        return self.count[semname]

    def wait(self, engine, semname, val):
        if val is None or val <= 0:
            return
        if self.active == engine:
            self.eng.wait_ge(self.sem_handles[semname], val)

    def dma(self, queue, out, in_, semname):
        instr = None
        if self.active == queue:
            instr = self.eng.dma_start(out=out, in_=in_)
        return self.inc(instr, semname, 16)

    phase = "?"
    pe_log = None

    def op(self, engine, method, semname=None, inck=1, **kw):
        if engine == "tensor" and self.pe_log is not None \
                and self.active == "tensor":
            self.pe_log.append(self.phase)
        instr = None
        if self.active == engine:
            instr = getattr(self.eng, method)(**kw)
        if semname is not None:
            return self.inc(instr, semname, inck)
        return None

    def barrier(self):
        vals = {}
        for e in ENGINES:
            instr = None
            if self.active == e:
                instr = self.eng.nop()
            vals[e] = self.inc(instr, "bar_" + e, 1)
        for e in ENGINES:
            for o in ENGINES:
                if o != e:
                    self.wait(e, "bar_" + o, vals[o])


class Slot:

    def __init__(self, ap):
        self.ap = ap
        self.last_read = None
        self.last_write = None

    def reset(self):
        self.last_read = None
        self.last_write = None

    def pre_write(self, ctx):
        if self.last_read is not None:
            ctx.wait("tensor", *self.last_read)

    def wrote(self, semname, count):
        self.last_write = (semname, count)

    def pre_read(self, ctx, engine):
        ctx.wait(engine, *self.last_write)

    def read_done(self, semname, count):
        self.last_read = (semname, count)


def build():
    nc = bass.Bass("TRN2", target_bir_lowering=False, debug=False)

    def din(name, shape, dt):
        return nc.dram_tensor(name, shape, dt, kind="ExternalInput").ap()

    x_in = din("x_in", [L, DM], F32)
    wx_p = din("wx_p", [DM, 288], F32)
    bg_in = din("bg_in", [128, KC], F32)
    dp_in = din("dp_in", [128, KC], F32)
    a_in = din("a_in", [128, H], F32)
    gpk = din("gpk", [SLICE_N, DM // 2], I32)
    upk = din("upk", [SLICE_N, DM // 2], I32)
    dpk = din("dpk", [DM, SLICE_J], I32)
    gsc = din("gsc", [64, SLICE_N], F32)
    usc = din("usc", [64, SLICE_N], F32)
    dsc = din("dsc", [64, DM], F32)
    idn_f = din("idn_f", [128, 128], F32)
    idn_b = din("idn_b", [128, 128], BF16)
    ones_b = din("ones_b", [128, 32], BF16)
    c22_in = din("c22_in", [128, 1], I32)
    wgf_in = din("wgf", [DM, DM], F32)
    wof_in = din("wof", [DM, DM], F32)
    out_t = nc.dram_tensor("out", [L, DM], F32, kind="ExternalOutput").ap()

    cc1_in = nc.dram_tensor("cc1_in", [2048, CC_COLS], BF16)
    cc1_out = nc.dram_tensor("cc1_out", [NCORES * 2048, CC_COLS], BF16,
                             addr_space="Shared")
    cc2_in = nc.dram_tensor("cc2_in", [1024, CC_COLS], BF16)
    cc2_out = nc.dram_tensor("cc2_out", [NCORES * 1024, CC_COLS], BF16,
                             addr_space="Shared")
    wgo_dram = nc.dram_tensor("wgo_dram", [2 * DM, DM], BF16)
    db_dram = nc.dram_tensor("db_dram", [H, T], F32)

    st = ExitStack()
    st.__enter__()
    sb = lambda n, s, d: st.enter_context(nc.sbuf_tensor(n, s, d))
    ps_ = lambda n, s, d: st.enter_context(nc.psum_tensor(n, s, d))

    xnT = sb("xnT", [128, KC * T], BF16)
    gate_sb = sb("gate_sb", [128, KC * T], BF16)
    yfull = sb("yfull", [128, KC * T], BF16)
    inter = sb("inter", [128, KC * T], BF16)
    acc = sb("acc", [128, KC * T], F32)
    B_sb = sb("B_sb", [128, T], F32)
    C_sb = sb("C_sb", [128, T], F32)
    d04 = sb("d04", [128, 4 * T], F32)
    a14 = sb("a14", [128, 4 * T], F32)
    dl4 = sb("dl4", [128, 4 * T], F32)
    dbbc4 = sb("dbbc4", [128, 4 * T], F32)
    db_sb = dbbc4[0:H, 0:T]
    sc_mh4 = sb("sc_mh4", [128, 4 * T], BF16)
    hcarry = sb("hcarry", [128, H], F32)
    ybuild = sb("ybuild", [128, T], F32)
    mr_sb = sb("mr_sb", [128, 2 * NT], F32)
    st1 = sb("st1", [128, 1], F32)
    st2 = sb("st2", [128, 1], F32)
    st3 = sb("st3", [128, 1], F32)
    wxw = sb("wxw", [128, KC * 288], BF16)
    wgo = sb("wgo", [128, 2 * 2048], BF16)
    gpan = sb("gpan", [128, 2 * 4096], BF16)
    upan = sb("upan", [128, 2 * 4096], BF16)
    wdp = sb("wdp", [128, 2 * 2048], BF16)
    gf1 = sb("gf1", [128, T], F32)
    gf2 = sb("gf2", [128, T], F32)
    gf3 = sb("gf3", [128, T], F32)
    gf4 = sb("gf4", [128, T], F32)
    ysb0 = sb("ysb0", [32, T], F32)
    ysb1 = sb("ysb1", [32, T], F32)
    ysb = [ysb0[:], ysb1[:]]
    sc_t1 = dl4[:, 0:T]
    sc_t2 = dl4[:, T:2 * T]
    gplist = [gpan[:, 0:4096], gpan[:, 4096:8192],
              d04[:].bitcast(BF16), dl4[:].bitcast(BF16)]
    uplist = [upan[:, 0:4096], upan[:, 4096:8192],
              a14[:].bitcast(BF16), dbbc4[:].bitcast(BF16)]
    bg_sb = sb("bg_sb", [128, KC], F32)
    dp_sb = sb("dp_sb", [128, KC], F32)
    a_sb = sb("a_sb", [128, H], F32)
    up_i32 = upan[:].bitcast(I32)
    idf_sb = up_i32[:, 3584:3712].bitcast(F32)
    idb_sb = sb("idb_sb", [128, 128], BF16)
    one_sb = sb("one_sb", [128, 32], BF16)
    c22 = sb("c22", [128, 1], I32)

    pk_i32 = gpan[:].bitcast(I32)
    acc_i32h = acc[:, 4096:8192].bitcast(I32)
    p2t_i32 = up_i32[:, 0:1024]
    dq_sc2 = up_i32[:, 1024:2048].bitcast(F32)
    dq_t1 = acc_i32h[:, 0:1024]
    dq_m2 = acc_i32h[:, 1024:2048]
    dq_mn = acc_i32h[:, 2048:3072]
    dq_sg = acc_i32h[:, 3072:4096]
    dq_bits = up_i32[:, 2048:3072]
    sqv = acc[:, 0:2048]
    inter_f32 = inter[:].bitcast(F32)
    xld = [inter_f32[:, 0:2048], inter_f32[:, 2048:4096]]
    gpan_f32 = gpan[:].bitcast(F32)
    xld_s = [gpan_f32[:, 0:2048], gpan_f32[:, 2048:4096]]
    xn_tm2 = [acc[:, 2048:3072].bitcast(BF16),
              acc[:, 3072:4096].bitcast(BF16)]
    yf_f32 = yfull[:].bitcast(F32)
    out_stage = [yf_f32[:, 0:2048], yf_f32[:, 2048:4096]]

    P = [ps_(f"P{i}", [128, 512], F32) for i in range(6)]
    PBa = ps_("PBa", [128, 512], F32)
    PBb = ps_("PBb", [128, 512], F32)
    slots = [Slot(p[:]) for p in P]
    slotA = Slot(PBa[:])
    slotB = Slot(PBb[:])
    slotA.b16 = PBa[:].bitcast(BF16)
    slotB.b16 = PBb[:].bitcast(BF16)
    ffn_pool = [slots[0], slots[1], slots[2], slots[5]]
    ffn_rot = [0]

    def next_bank():
        sl = ffn_pool[ffn_rot[0] % 4]
        ffn_rot[0] += 1
        return sl

    sem_names = (["CONST", "PK", "SCL", "GLD", "WCV", "WCV2", "TEN", "VEC", "ACT", "CC",
                  "DQS", "DQS2", "HC", "DBD", "YB",
                  "WGO0", "WGO1", "GP0", "GP1", "GP2", "GP3", "UP0", "UP1", "UP2", "UP3", "WD0", "WD1", "WD2",
                  "DBB0", "DBB1", "XLD0", "XLD1", "OUT0", "OUT1"]
                 + ["bar_" + e for e in ENGINES])
    sem_handles = {nm: st.enter_context(nc.semaphore(nm)) for nm in sem_names}
    ctx = Ctx(sem_handles)

    guards = {}

    def emit_consts(ctx):
        ctx.dma("sync", bg_sb[:], bg_in, "CONST")
        ctx.dma("sync", dp_sb[:], dp_in, "CONST")
        ctx.dma("sync", a_sb[:], a_in, "CONST")
        ctx.dma("sync", idf_sb[:], idn_f, "CONST")
        ctx.dma("sync", idb_sb[:], idn_b, "CONST")
        ctx.dma("sync", one_sb[:], ones_b, "CONST")
        ctx.dma("sync", c22[:], c22_in, "CONST")
        src = wx_p.rearrange("(c p) m -> p c m", p=128)
        dstv = wxw[:].rearrange("q (c m) -> q c m", m=288)
        ctx.dma("gpsimd", dstv, src, "GLD")
        ctx.dma("gpsimd", wgo_dram.ap()[0:DM, :], wgf_in, "WCV")
        ctx.dma("gpsimd", wgo_dram.ap()[DM:2 * DM, :], wof_in, "WCV2")
        return ctx.count["CONST"], ctx.count["GLD"]

    def emit_dequant(ctx, consts_ld, consts_gld):
        wout = [wdp[:, 0:1024], wdp[:, 1024:2048]]
        mats = [(gpk, gsc, cc1_in, 0, False), (upk, usc, cc1_in, 1024, False),
                (dpk, dsc, cc2_in, 0, True)]
        tr_rot = [0]
        for (mi, (pkt, sct, cc_t, row_base, is_down)) in enumerate(mats):
            for nq in range(4):
                for nt in range(4):
                    dst = pk_i32[:, nt * 1024:(nt + 1) * 1024]
                    if is_down:
                        srcr = pkt[(nq * 4 + nt) * 128:
                                   (nq * 4 + nt + 1) * 128, :]
                    else:
                        b0 = nq * 512 + (nt // 2) * 256
                        srcr = pkt[b0:b0 + 256, :].rearrange(
                            "(i two) k -> two i k", two=2)[nt % 2]
                    ctx.dma("sync", dst, srcr, "PK")
                pk_total = ctx.count["PK"]
                for Jc2 in range(4):
                    ctx.phase = "dqtr"
                    Jc = 2 * Jc2
                    g = guards.get("dqsc")
                    if g is not None:
                        ctx.wait("sync", *g)
                    for h in range(2):
                        ssrc = sct[8 * (Jc + h):8 * (Jc + h) + 8,
                                   nq * 512:(nq + 1) * 512] \
                            .unsqueeze(1).broadcast_to([8, 16, 512])
                        ctx.dma("sync", dq_sc2[:, h * 512:(h + 1) * 512],
                                ssrc, "SCL")
                    sc_mark = ctx.count["SCL"]
                    for h in range(2):
                        sl = slots[2 if h == 0 else 5]
                        sl.pre_write(ctx)
                        for nt in range(4):
                            ctx.wait("tensor", "PK", pk_total)
                            ctx.wait("tensor", "CONST", consts_ld)
                            ctx.op("tensor", "matmul",
                                   semname="TEN" if nt == 3 else None,
                                   out=sl.ap[:, nt * 128:(nt + 1) * 128],
                                   lhsT=pk_i32[:, nt * 1024 + (Jc + h) * 128:
                                               nt * 1024 + (Jc + h + 1) * 128]
                                   .bitcast(F32),
                                   rhs=idf_sb[:], is_transpose=True,
                                   start=True, stop=True)
                        sl.wrote("TEN", ctx.count["TEN"])
                        sl.pre_read(ctx, "vector")
                        ctx.op("vector", "tensor_copy", semname="VEC",
                               out=p2t_i32[:, h * 512:(h + 1) * 512],
                               in_=sl.ap[:].bitcast(I32))
                        sl.read_done("VEC", ctx.count["VEC"])
                    ctx.wait("vector", "SCL", sc_mark)
                    ctx.wait("vector", "CONST", consts_ld)
                    for par in range(2):
                        if par == 1:
                            ctx.op("vector", "tensor_scalar", out=dq_t1[:],
                                   in0=p2t_i32[:], scalar1=4, scalar2=None,
                                   op0=OP.logical_shift_right)
                        src32 = p2t_i32 if par == 0 else dq_t1
                        ctx.op("vector", "tensor_scalar", out=dq_m2[:],
                               in0=src32[:], scalar1=7, scalar2=None,
                               op0=OP.bitwise_and)
                        ctx.op("vector", "tensor_scalar", out=dq_mn[:],
                               in0=dq_m2[:], scalar1=252, scalar2=None,
                               op0=OP.add)
                        ctx.op("vector", "scalar_tensor_tensor", out=dq_mn[:],
                               in0=dq_m2[:], scalar=252, in1=dq_mn[:],
                               op0=OP.mult, op1=OP.min)
                        ctx.op("vector", "tensor_scalar", out=dq_sg[:],
                               in0=src32[:], scalar1=8, scalar2=28,
                               op0=OP.bitwise_and, op1=OP.logical_shift_left)
                        ctx.op("vector", "scalar_tensor_tensor",
                               out=dq_bits[:],
                               in0=dq_mn[:], scalar=c22[:, 0:1], in1=dq_sg[:],
                               op0=OP.logical_shift_left, op1=OP.bitwise_or)
                        dsem = "DQS" if par == 0 else "DQS2"
                        queue = "sync" if par == 0 else "scalar"
                        g = guards.get(("dqw", par))
                        if g is not None:
                            ctx.wait("vector", *g)
                        ctx.op("vector", "tensor_tensor", semname="VEC",
                               out=wout[par],
                               in0=dq_bits[:].bitcast(F32), in1=dq_sc2[:],
                               op=OP.mult)
                        wmark = ctx.count["VEC"]
                        ctx.wait(queue, "VEC", wmark)
                        rows = cc_t[row_base + 2 * nq * 128:
                                    row_base + (2 * nq + 2) * 128, :]
                        for h in range(2):
                            cdeq = 2 * (Jc + h) + par
                            wsrc = wout[par][:, h * 512:(h + 1) * 512]
                            if is_down:
                                for hh in range(2):
                                    dst = rows.rearrange("(J p) c -> p J c",
                                                         p=128)[
                                        :, :, hh * 2048 + cdeq * 128:
                                        hh * 2048 + (cdeq + 1) * 128]
                                    srcv = wsrc.rearrange(
                                        "p (J h c) -> p J h c", J=2, h=2)[
                                        :, :, hh, :]
                                    dqm = ctx.dma(queue, dst, srcv, dsem)
                            else:
                                dst = rows.rearrange("(J p) c -> p J c",
                                                     p=128)[
                                    :, :, cdeq * 256:(cdeq + 1) * 256]
                                srcv = wsrc.rearrange(
                                    "p (J c) -> p J c", J=2)
                                dqm = ctx.dma(queue, dst, srcv, dsem)
                        guards[("dqw", par)] = (dsem, dqm)
                    guards["dqsc"] = ("VEC", ctx.count["VEC"])
                ctx.wait("sync", "TEN", ctx.count["TEN"])
            if mi == 1:
                ctx.wait("gpsimd", "DQS", ctx.count["DQS"])
                ctx.wait("gpsimd", "DQS2", ctx.count["DQS2"])
                ctx.wait("gpsimd", "GLD", consts_gld)
                ctx.op("gpsimd", "collective_compute", semname="CC", inck=1,
                       kind="AllGather", op=OP.bypass,
                       replica_groups=[list(range(NCORES))],
                       ins=[cc1_in.ap()], outs=[cc1_out.ap()])
        ctx.wait("gpsimd", "DQS", ctx.count["DQS"])
        ctx.wait("gpsimd", "DQS2", ctx.count["DQS2"])
        ctx.op("gpsimd", "collective_compute", semname="CC", inck=1,
               kind="AllGather", op=OP.bypass,
               replica_groups=[list(range(NCORES))],
               ins=[cc2_in.ap()], outs=[cc2_out.ap()])

    def emit_stats_xn(ctx, p, consts_ld):
        for tt in range(NT):
            buf = tt % 2
            row0 = p * T + tt * 128
            g = guards.get("units_done")
            if g is not None:
                ctx.wait("sync", *g)
            else:
                ctx.wait("sync", "TEN", ctx.count["TEN"])
            g = guards.get(("xld", buf))
            if g is not None:
                ctx.wait("sync", *g)
            ctx.dma("sync", xld_s[buf], x_in[row0:row0 + 128, :],
                    "XLD%d" % buf)
            ctx.wait("vector", "XLD%d" % buf, ctx.count["XLD%d" % buf])
            ctx.wait("vector", "ACT", ctx.count["ACT"])
            ctx.op("vector", "tensor_tensor", out=sqv[:], in0=xld_s[buf],
                   in1=xld_s[buf], op=OP.mult)
            ctx.op("vector", "reduce_sum", out=st1[:], in_=xld_s[buf],
                   axis=AX.X)
            ctx.op("vector", "reduce_sum", out=st2[:], in_=sqv[:], axis=AX.X)
            ctx.op("vector", "tensor_scalar", out=mr_sb[:, 2 * tt:2 * tt + 1],
                   in0=st1[:], scalar1=1.0 / DM, scalar2=None, op0=OP.mult)
            ctx.op("vector", "tensor_scalar", out=st2[:], in0=st2[:],
                   scalar1=1.0 / DM, scalar2=None, op0=OP.mult)
            ctx.op("vector", "tensor_tensor", out=st3[:],
                   in0=mr_sb[:, 2 * tt:2 * tt + 1],
                   in1=mr_sb[:, 2 * tt:2 * tt + 1], op=OP.mult)
            ctx.op("vector", "tensor_tensor", out=st2[:], in0=st2[:], in1=st3[:],
                   op=OP.subtract)
            ctx.op("vector", "tensor_scalar", out=st2[:], in0=st2[:],
                   scalar1=1e-5, scalar2=None, op0=OP.add, semname="VEC")
            ctx.wait("scalar", "VEC", ctx.count["VEC"])
            ctx.op("scalar", "activation", out=st3[:], in_=st2[:], func=AF.Ln)
            ctx.op("scalar", "activation", out=mr_sb[:, 2 * tt + 1:2 * tt + 2],
                   in_=st3[:], func=AF.Exp, scale=-0.5, semname="ACT")
            ctx.wait("vector", "ACT", ctx.count["ACT"])
            xn_tm = xn_tm2[tt % 2]
            g = guards.get(("xn_tm", tt % 2))
            if g is not None:
                ctx.wait("vector", *g)
            ctx.op("vector", "tensor_scalar", out=xn_tm[:], in0=xld_s[buf],
                   scalar1=mr_sb[:, 2 * tt:2 * tt + 1],
                   scalar2=mr_sb[:, 2 * tt + 1:2 * tt + 2],
                   op0=OP.subtract, op1=OP.mult, semname="VEC")
            xnm = ctx.count["VEC"]
            guards[("xld", buf)] = ("VEC", xnm)
            ctx.wait("tensor", "VEC", xnm)
            ctx.wait("tensor", "CONST", consts_ld)
            ctx.phase = "stats%d" % tt
            for c in range(KC):
                J, par = c // 2, c % 2
                sl = slotA if c % 2 == 0 else slotB
                off = c // 2
                if off == 0:
                    sl.pre_write(ctx)
                src_ap = xn_tm.rearrange("q (j two) -> q j two", two=2)[
                    :, 128 * J:128 * (J + 1), par:par + 1].squeeze(-1)
                last = (c >= KC - 2)
                ctx.op("tensor", "matmul", semname="TEN" if last else None,
                       out=sl.b16[:, off * 128:(off + 1) * 128],
                       lhsT=src_ap, rhs=idb_sb[:], is_transpose=True,
                       start=True, stop=True)
                if last:
                    sl.wrote("TEN", ctx.count["TEN"])
            guards[("xn_tm", tt % 2)] = ("TEN", ctx.count["TEN"])
            slotA.pre_read(ctx, "scalar")
            slotB.pre_read(ctx, "scalar")
            for c in range(KC):
                sl = slotA if c % 2 == 0 else slotB
                off = c // 2
                ctx.op("scalar", "copy",
                       semname="ACT" if c == KC - 1 else None,
                       out=xnT[:, c * T + tt * 128: c * T + (tt + 1) * 128],
                       in_=sl.b16[:, off * 128:(off + 1) * 128])
            slotA.read_done("ACT", ctx.count["ACT"])
            slotB.read_done("ACT", ctx.count["ACT"])

    def dma_wgo(ctx, which, oc, buf, consts_gld, qw):
        ctx.wait(qw, "WCV" if which == 0 else "WCV2", 16)
        g = guards.get(("wgo", buf))
        if g is not None:
            ctx.wait(qw, *g)
        sem = "WGO%d" % buf
        r0 = which * DM + oc * 128
        ctx.dma(qw, wgo[:, buf * 2048:(buf + 1) * 2048],
                wgo_dram.ap()[r0:r0 + 128, :], sem)
        return (sem, ctx.count[sem])

    def emit_merged(ctx, p, consts_ld, consts_gld):
        xn_done = ctx.count["ACT"]
        ctx.wait("tensor", "ACT", xn_done)
        ctx.wait("tensor", "GLD", consts_gld)
        q_scan = "scalar" if p == 0 else "gpsimd"

        ctx.phase = "ssm"
        trips = ((0, 128, slots[2]), (128, 128, slots[5]), (256, 32, slotB))
        for (col0, ncols, sl) in trips:
            sl.pre_write(ctx)
            for kc in range(KC):
                last = kc == KC - 1
                ctx.op("tensor", "matmul", semname="TEN" if last else None,
                       out=sl.ap[0:ncols, :],
                       lhsT=wxw[:, kc * 288 + col0: kc * 288 + col0 + ncols],
                       rhs=xnT[:, kc * T:(kc + 1) * T],
                       start=(kc == 0), stop=last)
            sl.wrote("TEN", ctx.count["TEN"])
        for ((col0, ncols, sl), dst) in zip(trips,
                                            (B_sb[:], C_sb[:], db_sb[:])):
            sl.pre_read(ctx, "vector")
            ctx.op("vector", "tensor_copy", semname="VEC", out=dst,
                   in_=sl.ap[0:ncols, :])
            sl.read_done("VEC", ctx.count["VEC"])
        db_evac = ctx.count["VEC"]
        ctx.wait(q_scan, "VEC", db_evac)
        ctx.wait(q_scan, "DBB0", ctx.count["DBB0"])
        ctx.dma(q_scan, db_dram.ap(), db_sb[:], "DBD")
        ctx.wait(q_scan, "DBD", ctx.count["DBD"])
        for q in range(4):
            ctx.dma(
                q_scan, dbbc4[:, q * T:(q + 1) * T],
                db_dram.ap()[q:q + 1, :].partition_broadcast(128).squeeze(1),
                "DBB0")
        ctx.wait("vector", "HC", p)

        st = {"group": 0, "yg": 0, "wg": 0, "wo": 0, "slot": 0, "yf": None,
              "ymm_done": None, "mh_mark": None,
              "dbbc_mark": ctx.count["DBB0"]}

        def emit_wgo_group(which, oc):
            ctx.phase = "w%dg%d" % (which, oc)
            buf = oc % 2
            gm = dma_wgo(ctx, which, oc, buf, consts_gld, "sync")
            sl = slots[3 + buf]
            sl.pre_write(ctx)
            if which == 1 and oc == 0:
                ctx.wait("tensor", "VEC", st["yf"])
            ctx.wait("tensor", *gm)
            rhs_t = xnT if which == 0 else yfull
            for kc in range(KC):
                last = kc == KC - 1
                ctx.op("tensor", "matmul", semname="TEN" if last else None,
                       out=sl.ap[:],
                       lhsT=wgo[:, buf * 2048 + kc * 128:
                                buf * 2048 + (kc + 1) * 128],
                       rhs=rhs_t[:, kc * T:(kc + 1) * T],
                       start=(kc == 0), stop=last)
            sl.wrote("TEN", ctx.count["TEN"])
            guards[("wgo", buf)] = ("TEN", ctx.count["TEN"])
            sl.pre_read(ctx, "scalar")
            if which == 0:
                ctx.wait("scalar", "CONST", consts_ld)
                ctx.op("scalar", "activation", semname="ACT",
                       out=gate_sb[:, oc * T:(oc + 1) * T], in_=sl.ap[:],
                       func=AF.Sigmoid, bias=bg_sb[:, oc:oc + 1])
            else:
                ctx.op("scalar", "copy", semname="ACT",
                       out=gate_sb[:, oc * T:(oc + 1) * T], in_=sl.ap[:])
            sl.read_done("ACT", ctx.count["ACT"])

        def emit_scan_group():
            g4 = st["group"]
            st["group"] += 1
            h0 = 4 * g4
            ctx.wait("vector", "DBB0", st["dbbc_mark"])
            for q in range(4):
                ctx.op("vector", "tensor_tensor",
                       out=d04[:, q * T:(q + 1) * T], in0=B_sb[:],
                       in1=dbbc4[:, q * T:(q + 1) * T], op=OP.add,
                       semname="VEC" if q == 3 else None)
            d0m = ctx.count["VEC"]
            if h0 + 4 < H:
                ctx.wait(q_scan, "VEC", d0m)
                for q in range(4):
                    nh = h0 + 4 + q
                    ctx.dma(
                        q_scan, dbbc4[:, q * T:(q + 1) * T],
                        db_dram.ap()[nh:nh + 1, :].partition_broadcast(128)
                        .squeeze(1), "DBB0")
                st["dbbc_mark"] = ctx.count["DBB0"]
            ctx.wait("scalar", "VEC", d0m)
            ctx.op("scalar", "activation", out=a14[:], in_=d04[:], func=AF.Abs)
            ctx.op("scalar", "activation", out=a14[:], in_=a14[:],
                   func=AF.Exp, scale=-1.0)
            ctx.op("scalar", "activation", out=a14[:], in_=a14[:],
                   func=AF.Ln, bias=1.0, semname="ACT")
            lnm = ctx.count["ACT"]
            ctx.op("vector", "tensor_scalar", out=dl4[:], in0=d04[:],
                   scalar1=0.0, scalar2=None, op0=OP.max)
            ctx.wait("vector", "ACT", lnm)
            ctx.op("vector", "tensor_tensor", out=dl4[:], in0=dl4[:],
                   in1=a14[:], op=OP.add, semname="VEC")
            dlm = ctx.count["VEC"]
            ctx.wait("scalar", "VEC", dlm)
            ctx.wait("scalar", "CONST", consts_ld)
            for q in range(4):
                ctx.op("scalar", "activation", out=d04[:, q * T:(q + 1) * T],
                       in_=dl4[:, q * T:(q + 1) * T], func=AF.Exp,
                       scale=a_sb[:, h0 + q:h0 + q + 1],
                       semname="ACT" if q == 3 else None)
            expm = ctx.count["ACT"]
            ctx.op("vector", "tensor_tensor", out=a14[:], in0=dl4[:],
                   in1=dl4[:], op=OP.mult)
            ctx.wait("vector", "ACT", expm)
            for q in range(4):
                h = h0 + q
                init = 0.0 if p == 0 else hcarry[:, h:h + 1]
                ctx.op("vector", "tensor_tensor_scan",
                       out=dl4[:, q * T:(q + 1) * T],
                       data0=d04[:, q * T:(q + 1) * T],
                       data1=a14[:, q * T:(q + 1) * T],
                       initial=init, op0=OP.mult, op1=OP.add)
                g = guards.get(("mh", q))
                if g is not None:
                    ctx.wait("vector", *g)
                ctx.op("vector", "tensor_tensor",
                       out=sc_mh4[:, q * T:(q + 1) * T],
                       in0=dl4[:, q * T:(q + 1) * T],
                       in1=C_sb[:], op=OP.mult,
                       semname="VEC" if q == 3 else None)
                ctx.op("vector", "tensor_copy", out=hcarry[:, h:h + 1],
                       in_=dl4[:, q * T + T - 1:q * T + T])
            st["mh_mark"] = ctx.count["VEC"]

        def emit_y_group():
            ctx.phase = "y%d" % st["yg"]
            g4 = st["yg"]
            st["yg"] += 1
            if g4 == 0:
                ctx.wait("vector", "OUT0", ctx.count["OUT0"])
                ctx.wait("vector", "OUT1", ctx.count["OUT1"])
            ctx.wait("tensor", "VEC", st["mh_mark"])
            for q in range(4):
                mh_ap = sc_mh4[:, q * T:(q + 1) * T]
                sl = slotA if q % 2 == 0 else slotB
                sl.pre_write(ctx)
                ctx.op("tensor", "matmul", semname="TEN",
                       out=sl.ap[0:32, :], lhsT=one_sb[:, 0:32], rhs=mh_ap,
                       start=True, stop=True)
                guards[("mh", q)] = ("TEN", ctx.count["TEN"])
                sl.wrote("TEN", ctx.count["TEN"])
                sl.pre_read(ctx, "scalar")
                ctx.wait("scalar", "YB", ctx.count["YB"])
                ctx.op("scalar", "copy", semname="ACT",
                       out=ysb[q % 2], in_=sl.ap[0:32, :])
                sl.read_done("ACT", ctx.count["ACT"])
                if q == 0:
                    g = guards.get("ybuild")
                    if g is not None:
                        ctx.wait(q_scan, *g)
                ctx.wait(q_scan, "ACT", ctx.count["ACT"])
                ctx.dma(q_scan, ybuild[32 * q:32 * (q + 1), :], ysb[q % 2],
                        "YB")
            ctx.wait("vector", "YB", ctx.count["YB"])
            for c in (2 * g4, 2 * g4 + 1):
                ctx.op("vector", "tensor_scalar", out=sc_t1[:],
                       in0=xnT[:, c * T:(c + 1) * T],
                       scalar1=dp_sb[:, c:c + 1], scalar2=None,
                       op0=OP.mult)
                ctx.op("vector", "tensor_tensor", out=sc_t2[:],
                       in0=sc_t1[:], in1=ybuild[:], op=OP.add)
                ctx.op("vector", "tensor_tensor", semname="VEC",
                       out=yfull[:, c * T:(c + 1) * T],
                       in0=sc_t2[:], in1=gate_sb[:, c * T:(c + 1) * T],
                       op=OP.mult)
            guards["ybuild"] = ("VEC", ctx.count["VEC"])
            if g4 == H // 4 - 1:
                ctx.op("vector", "nop", semname="HC")
                st["yf"] = ctx.count["VEC"]
                st["ymm_done"] = ctx.count["TEN"]

        def aux():
            s = st["slot"]
            st["slot"] += 1
            if st["wg"] < KC and s % 2 == 1:
                emit_wgo_group(0, st["wg"])
                st["wg"] += 1
            if s % 4 == 3 and st["yg"] < st["group"]:
                emit_y_group()
            if s % 4 == 3 and st["group"] < H // 4:
                emit_scan_group()
            if st["yg"] == H // 4 and st["wo"] < KC and s % 2 == 0:
                emit_wgo_group(1, st["wo"])
                st["wo"] += 1

        inter_marks = {}
        for ns in range(NCORES):
            down_guard = guards.get("downTEN")
            for J2 in range(8):
                u = ns * 8 + J2
                if u < 18:
                    buf = u % 2
                else:
                    buf = (2, 3, 0, 1)[(u - 18) % 4]
                pm = {}
                for (plist, roff, nm, fam, q_) in (
                        (gplist, 0, "gp", "GP", "sync"),
                        (uplist, 1024, "up", "UP", "sync")):
                    g = guards.get((nm, buf))
                    if g is not None:
                        ctx.wait(q_, *g)
                    if nm == "gp" and buf < 2:
                        g = guards.get(("xld", buf))
                        if g is not None:
                            ctx.wait(q_, *g)
                    if u in (18, 19):
                        ctx.wait(q_, "VEC", st["yf"])
                    ctx.wait(q_, "CC", 1)
                    r0 = ns * 2048 + roff + J2 * 128
                    src = cc1_out.ap()[r0:r0 + 128, :]
                    dst = plist[buf]
                    sem = fam + str(buf)
                    ctx.dma(q_, dst, src, sem)
                    pm[nm] = (sem, ctx.count[sem])
                for par in range(2):
                    ci = J2 * 2 + par
                    ctx.phase = "u%d.%d.%d" % (ns, J2, par)
                    subslots = []
                    for (plist, sid, nm) in ((gplist, 0, "gp"),
                                             (uplist, 1, "up")):
                        sl = next_bank()
                        subslots.append(sl)
                        sl.pre_write(ctx)
                        ctx.wait("tensor", *pm[nm])
                        pan_ap = plist[buf]
                        for kc in range(KC):
                            last = kc == KC - 1
                            c0 = kc * 256 + par * 128
                            lhs = pan_ap[:, c0:c0 + 128]
                            ctx.op("tensor", "matmul",
                                   semname="TEN" if last else None,
                                   out=sl.ap[:], lhsT=lhs,
                                   rhs=xnT[:, kc * T:(kc + 1) * T],
                                   start=(kc == 0), stop=last)
                        sl.wrote("TEN", ctx.count["TEN"])
                    guards[("gp", buf)] = ("TEN", ctx.count["TEN"])
                    guards[("up", buf)] = ("TEN", ctx.count["TEN"])
                    slg, slu = subslots
                    slg.pre_read(ctx, "scalar")
                    slu.pre_read(ctx, "scalar")
                    g = guards.get("gelu")
                    if g is not None:
                        ctx.wait("scalar", *g)
                    ctx.op("scalar", "copy", out=gf1[:], in_=slg.ap[:])
                    ctx.op("scalar", "activation", out=gf2[:], in_=slg.ap[:],
                           func=AF.Square)
                    ctx.op("scalar", "activation", out=gf3[:], in_=slu.ap[:],
                           func=AF.Copy, scale=0.5, semname="ACT")
                    gm = ctx.count["ACT"]
                    slg.read_done("ACT", gm)
                    slu.read_done("ACT", gm)
                    ctx.wait("vector", "ACT", gm)
                    ctx.op("vector", "tensor_scalar", out=gf2[:], in0=gf2[:],
                           scalar1=GELU_C1, scalar2=GELU_C0,
                           op0=OP.mult, op1=OP.add)
                    ctx.op("vector", "tensor_tensor", out=gf2[:], in0=gf1[:],
                           in1=gf2[:], op=OP.mult, semname="VEC")
                    ctx.wait("scalar", "VEC", ctx.count["VEC"])
                    ctx.op("scalar", "activation", out=gf2[:], in_=gf2[:],
                           func=AF.Tanh, semname="ACT")
                    ctx.wait("vector", "ACT", ctx.count["ACT"])
                    ctx.op("vector", "tensor_scalar", out=gf2[:], in0=gf2[:],
                           scalar1=1.0, scalar2=None, op0=OP.add)
                    ctx.op("vector", "tensor_tensor", out=gf4[:], in0=gf1[:],
                           in1=gf3[:], op=OP.mult)
                    if down_guard is not None and J2 == 0 and par == 0:
                        ctx.wait("vector", *down_guard)
                    ctx.op("vector", "tensor_tensor", semname="VEC",
                           out=inter[:, ci * T:(ci + 1) * T],
                           in0=gf4[:], in1=gf2[:], op=OP.mult)
                    inter_marks[ci] = ctx.count["VEC"]
                    guards["gelu"] = ("VEC", ctx.count["VEC"])
                    aux()
                if ns == NCORES - 1 and J2 == 7:
                    guards["units_done"] = ("TEN", ctx.count["TEN"])
            wdlist = [wdp[:, 0:2048], wdp[:, 2048:4096], sc_mh4[:]]
            for ht in range(KC):
                buf = ht % 2 if ns == 0 else (ns * KC + ht) % 3
                ctx.phase = "dn%d.%d" % (ns, ht)
                g = guards.get(("wd", buf))
                if g is not None:
                    ctx.wait("gpsimd", *g)
                if buf == 2:
                    ctx.wait("gpsimd", "TEN", st["ymm_done"])
                ctx.wait("gpsimd", "CC", 2)
                r0 = ns * 1024 + (ht // 2) * 128
                src = cc2_out.ap()[r0:r0 + 128,
                                  (ht % 2) * 2048:(ht % 2 + 1) * 2048]
                dst = wdlist[buf]
                sem = "WD%d" % buf
                ctx.dma("gpsimd", dst, src, sem)
                sl = next_bank()
                sl.pre_write(ctx)
                ctx.wait("tensor", sem, ctx.count[sem])
                if ht == 0:
                    ctx.wait("tensor", "VEC", inter_marks[15])
                for cn in range(KC):
                    last = cn == KC - 1
                    ctx.op("tensor", "matmul", semname="TEN" if last else None,
                           out=sl.ap[:],
                           lhsT=wdlist[buf][:, cn * 128:(cn + 1) * 128],
                           rhs=inter[:, cn * T:(cn + 1) * T],
                           start=(cn == 0), stop=last)
                sl.wrote("TEN", ctx.count["TEN"])
                guards[("wd", buf)] = ("TEN", ctx.count["TEN"])
                if ns == 0:
                    sl.pre_read(ctx, "scalar")
                    ctx.op("scalar", "copy", semname="ACT",
                           out=acc[:, ht * T:(ht + 1) * T], in_=sl.ap[:])
                    sl.read_done("ACT", ctx.count["ACT"])
                else:
                    sl.pre_read(ctx, "vector")
                    ctx.op("vector", "tensor_tensor", semname="VEC",
                           out=acc[:, ht * T:(ht + 1) * T],
                           in0=acc[:, ht * T:(ht + 1) * T], in1=sl.ap[:],
                           op=OP.add)
                    sl.read_done("VEC", ctx.count["VEC"])
                aux()
            guards["downTEN"] = ("TEN", ctx.count["TEN"])
        assert st["group"] == H // 4 and st["yg"] == H // 4 \
            and st["wg"] == KC and st["wo"] == KC, st

    def emit_combine(ctx, p):
        acc_v = ctx.count["VEC"]
        acc_a = ctx.count["ACT"]
        for k4 in range(KC // 4):
            c0 = k4 * 4 * T
            ctx.wait("scalar", "VEC", acc_v)
            ctx.wait("scalar", "TEN", ctx.count["TEN"])
            g = guards.get("sgf")
            if g is not None:
                ctx.wait("scalar", *g)
            ctx.op("scalar", "activation", semname="ACT", out=d04[:],
                   in_=acc[:, c0:c0 + 4 * T], func=AF.Sigmoid)
            ctx.wait("vector", "ACT", ctx.count["ACT"])
            ctx.op("vector", "tensor_tensor", out=a14[:],
                   in0=acc[:, c0:c0 + 4 * T], in1=d04[:], op=OP.mult)
            ctx.op("vector", "tensor_tensor", semname="VEC",
                   out=gate_sb[:, c0:c0 + 4 * T],
                   in0=a14[:], in1=gate_sb[:, c0:c0 + 4 * T],
                   op=OP.mult)
            guards["sgf"] = ("VEC", ctx.count["VEC"])
        sg_done = ctx.count["VEC"]
        ctx.wait("tensor", "VEC", sg_done)
        xld_c = [dl4[:], dbbc4[:]]
        for tt in range(NT):
            buf = tt % 2
            row0 = p * T + tt * 128
            g = guards.get(("xldc", buf))
            if g is not None:
                ctx.wait("sync", *g)
            g = guards.get("units_done")
            if g is not None:
                ctx.wait("sync", *g)
            ctx.dma("sync", xld_c[buf], x_in[row0:row0 + 128, :],
                    "XLD%d" % buf)
            g = guards.get(("ostage", buf))
            if g is not None:
                ctx.wait("vector", *g)
            for ht in range(KC):
                sl = slotA if ht % 2 == 0 else slotB
                ctx.phase = "comb%d.%d" % (tt, ht)
                off = ht // 2
                sl.pre_write(ctx)
                ctx.op("tensor", "matmul", semname="TEN",
                       out=sl.b16[:, off * 128:(off + 1) * 128],
                       lhsT=gate_sb[:, ht * T + tt * 128: ht * T + (tt + 1) * 128],
                       rhs=idb_sb[:], is_transpose=True, start=True, stop=True)
                sl.wrote("TEN", ctx.count["TEN"])
                sl.pre_read(ctx, "vector")
                ctx.wait("vector", "XLD%d" % buf, ctx.count["XLD%d" % buf])
                ctx.op("vector", "tensor_tensor", semname="VEC",
                       out=out_stage[buf][:, ht * 128:(ht + 1) * 128],
                       in0=xld_c[buf][:, ht * 128:(ht + 1) * 128],
                       in1=sl.b16[:, off * 128:(off + 1) * 128], op=OP.add)
                sl.read_done("VEC", ctx.count["VEC"])
            vm = ctx.count["VEC"]
            ctx.wait("sync", "VEC", vm)
            sem = "OUT%d" % buf
            ctx.dma("sync", out_t[row0:row0 + 128, :], out_stage[buf], sem)
            guards[("xldc", buf)] = (sem, ctx.count[sem])
            guards[("ostage", buf)] = (sem, ctx.count[sem])

    def schedule(ctx):
        for sl in slots + [slotA, slotB]:
            sl.reset()
        guards.clear()
        ffn_rot[0] = 0
        consts_ld, consts_gld = emit_consts(ctx)
        emit_dequant(ctx, consts_ld, consts_gld)
        for p in range(NPASS):
            emit_stats_xn(ctx, p, consts_ld)
            emit_merged(ctx, p, consts_ld, consts_gld)
            emit_combine(ctx, p)
        ctx.wait("sync", "OUT0", ctx.count["OUT0"])
        ctx.wait("sync", "OUT1", ctx.count["OUT1"])

    with nc.Block() as block:
        @block.sync
        def _(eng):
            ctx.begin("sync", eng)
            schedule(ctx)

        @block.tensor
        def _(eng):
            ctx.begin("tensor", eng)
            schedule(ctx)

        @block.vector
        def _(eng):
            ctx.begin("vector", eng)
            schedule(ctx)

        @block.scalar
        def _(eng):
            ctx.begin("scalar", eng)
            schedule(ctx)

        @block.gpsimd
        def _(eng):
            ctx.begin("gpsimd", eng)
            schedule(ctx)

    st.close()
    return nc


def prep_inputs(inputs):
    perm = build_perm()
    x = np.asarray(inputs["L0_attn_x"], dtype=np.float32)
    Wx = np.asarray(inputs["L0_attn_Wx"], dtype=np.float32)
    Wg = np.asarray(inputs["L0_attn_Wg"], dtype=np.float32)
    bg = np.asarray(inputs["L0_attn_bg"], dtype=np.float32)
    Wo = np.asarray(inputs["L0_attn_Wo"], dtype=np.float32)
    A = np.asarray(inputs["L0_attn_A"], dtype=np.float32)
    Dp = np.asarray(inputs["L0_attn_Dp"], dtype=np.float32)
    gpkf = np.asarray(inputs["L0_ffn_gate_W_packed"], dtype=np.int32)
    gscales = np.asarray(inputs["L0_ffn_gate_scales"], dtype=np.float32)
    upkf = np.asarray(inputs["L0_ffn_up_W_packed"], dtype=np.int32)
    uscales = np.asarray(inputs["L0_ffn_up_scales"], dtype=np.float32)
    dpkf = np.asarray(inputs["L0_ffn_down_W_packed"], dtype=np.int32)
    dscales = np.asarray(inputs["L0_ffn_down_scales"], dtype=np.float32)

    wx_p = np.ascontiguousarray(Wx[perm])
    wg_p = np.ascontiguousarray(Wg[perm][:, perm])
    wo_p = np.ascontiguousarray(Wo[perm])
    wgf = np.ascontiguousarray(
        wg_p.reshape(KC, 128, KC, 128).transpose(2, 1, 0, 3).reshape(DM, DM))
    wof = np.ascontiguousarray(
        wo_p.reshape(KC, 128, KC, 128).transpose(2, 1, 0, 3).reshape(DM, DM))
    bg_in = np.ascontiguousarray(bg[perm].reshape(KC, 128).T)
    dp_in = np.ascontiguousarray(np.repeat(Dp, HD)[perm].reshape(KC, 128).T)
    a_in = np.ascontiguousarray(A.T)
    mlp_perm = np.arange(SLICE_N).reshape(4, 2, 128, 2) \
        .transpose(0, 1, 3, 2).reshape(-1)
    idn_f = np.eye(128, dtype=np.float32)
    idn_b = np.eye(128, dtype=np.float32).astype(ml_dtypes.bfloat16)
    ones_b = np.ones((128, 32), dtype=np.float32).astype(ml_dtypes.bfloat16)
    c22_in = np.full((128, 1), 22, dtype=np.int32)
    dscT = np.ascontiguousarray(dscales.T)

    in_maps = []
    for i in range(NCORES):
        in_maps.append({
            "x_in": np.ascontiguousarray(x[i]),
            "wx_p": wx_p,
            "bg_in": bg_in, "dp_in": dp_in, "a_in": a_in,
            "gpk": np.ascontiguousarray(gpkf[i * SLICE_N:(i + 1) * SLICE_N]),
            "upk": np.ascontiguousarray(upkf[i * SLICE_N:(i + 1) * SLICE_N]),
            "dpk": np.ascontiguousarray(dpkf[:, i * SLICE_J:(i + 1) * SLICE_J]),
            "gsc": np.ascontiguousarray(
                gscales[i * SLICE_N:(i + 1) * SLICE_N].T[:, mlp_perm]),
            "usc": np.ascontiguousarray(
                uscales[i * SLICE_N:(i + 1) * SLICE_N].T[:, mlp_perm]),
            "dsc": np.ascontiguousarray(dscT[i * 64:(i + 1) * 64]),
            "idn_f": idn_f, "idn_b": idn_b, "ones_b": ones_b, "c22_in": c22_in,
            "wgf": wgf, "wof": wof,
        })
    return in_maps


def kernel(**inputs):
    in_maps = prep_inputs(inputs)
    nc = build()
    res = run_bass_kernel_spmd(nc, in_maps, core_ids=list(range(NCORES)))
    out = np.stack([np.asarray(res.results[i]["out"]) for i in range(NCORES)],
                   axis=0)
    return out.astype(np.float32)


# version 33
# speedup vs baseline: 1.0667x; 1.0667x over previous
import numpy as np
import ml_dtypes
from contextlib import ExitStack

import concourse.bass as bass
import concourse.mybir as mybir
from concourse.bass_utils import run_bass_kernel_spmd

F32 = mybir.dt.float32
BF16 = mybir.dt.bfloat16
I32 = mybir.dt.int32
OP = mybir.AluOpType
AF = mybir.ActivationFunctionType
AX = mybir.AxisListType

NCORES = 8
B, L, DM = 8, 2048, 2048
H, N, HD = 32, 128, 64
MLP = 16384
T = 512
NPASS = L // T
KC = DM // 128
NT = T // 128
SLICE_N = MLP // NCORES
SLICE_J = SLICE_N // 2
GELU_C0 = 0.7978845608028654
GELU_C1 = 0.044715 * GELU_C0
CC_ROWS = 3072
CC_COLS = 4096
ENGINES = ("sync", "tensor", "vector", "scalar", "gpsimd")


def build_perm():
    perm = np.empty(DM, dtype=np.int64)
    for c in range(KC):
        J, par = c // 2, c % 2
        perm[c * 128:(c + 1) * 128] = 256 * J + 2 * np.arange(128) + par
    return perm


class Ctx:

    def __init__(self, sem_handles):
        self.sem_handles = sem_handles

    def begin(self, name, eng):
        self.active = name
        self.eng = eng
        self.count = {k: 0 for k in self.sem_handles}

    def inc(self, instr, semname, k=1):
        if instr is not None:
            instr.then_inc(self.sem_handles[semname], k)
        self.count[semname] += k
        return self.count[semname]

    def wait(self, engine, semname, val):
        if val is None or val <= 0:
            return
        if self.active == engine:
            self.eng.wait_ge(self.sem_handles[semname], val)

    def dma(self, queue, out, in_, semname):
        instr = None
        if self.active == queue:
            instr = self.eng.dma_start(out=out, in_=in_)
        return self.inc(instr, semname, 16)

    phase = "?"
    pe_log = None

    def op(self, engine, method, semname=None, inck=1, **kw):
        if engine == "tensor" and self.pe_log is not None \
                and self.active == "tensor":
            self.pe_log.append(self.phase)
        instr = None
        if self.active == engine:
            instr = getattr(self.eng, method)(**kw)
        if semname is not None:
            return self.inc(instr, semname, inck)
        return None

    def barrier(self):
        vals = {}
        for e in ENGINES:
            instr = None
            if self.active == e:
                instr = self.eng.nop()
            vals[e] = self.inc(instr, "bar_" + e, 1)
        for e in ENGINES:
            for o in ENGINES:
                if o != e:
                    self.wait(e, "bar_" + o, vals[o])


class Slot:

    def __init__(self, ap):
        self.ap = ap
        self.last_read = None
        self.last_write = None

    def reset(self):
        self.last_read = None
        self.last_write = None

    def pre_write(self, ctx):
        if self.last_read is not None:
            ctx.wait("tensor", *self.last_read)

    def wrote(self, semname, count):
        self.last_write = (semname, count)

    def pre_read(self, ctx, engine):
        ctx.wait(engine, *self.last_write)

    def read_done(self, semname, count):
        self.last_read = (semname, count)


def build():
    nc = bass.Bass("TRN2", target_bir_lowering=False, debug=False)

    def din(name, shape, dt):
        return nc.dram_tensor(name, shape, dt, kind="ExternalInput").ap()

    x_in = din("x_in", [L, DM], F32)
    wx_p = din("wx_p", [DM, 288], F32)
    bg_in = din("bg_in", [128, KC], F32)
    dp_in = din("dp_in", [128, KC], F32)
    a_in = din("a_in", [128, H], F32)
    gpk = din("gpk", [SLICE_N, DM // 2], I32)
    upk = din("upk", [SLICE_N, DM // 2], I32)
    dpk = din("dpk", [DM, SLICE_J], I32)
    gsc = din("gsc", [64, SLICE_N], F32)
    usc = din("usc", [64, SLICE_N], F32)
    dsc = din("dsc", [64, DM], F32)
    idn_f = din("idn_f", [128, 128], F32)
    idn_b = din("idn_b", [128, 128], BF16)
    ones_b = din("ones_b", [128, 32], BF16)
    c22_in = din("c22_in", [128, 1], I32)
    wgf_in = din("wgf", [DM, DM], F32)
    wof_in = din("wof", [DM, DM], F32)
    out_t = nc.dram_tensor("out", [L, DM], F32, kind="ExternalOutput").ap()

    cc1_in = nc.dram_tensor("cc1_in", [2048, CC_COLS], BF16)
    cc1_out = nc.dram_tensor("cc1_out", [NCORES * 2048, CC_COLS], BF16,
                             addr_space="Shared")
    cc2_in = nc.dram_tensor("cc2_in", [1024, CC_COLS], BF16)
    cc2_out = nc.dram_tensor("cc2_out", [NCORES * 1024, CC_COLS], BF16,
                             addr_space="Shared")
    wgo_dram = nc.dram_tensor("wgo_dram", [2 * DM, DM], BF16)
    db_dram = nc.dram_tensor("db_dram", [H, T], F32)

    st = ExitStack()
    st.__enter__()
    sb = lambda n, s, d: st.enter_context(nc.sbuf_tensor(n, s, d))
    ps_ = lambda n, s, d: st.enter_context(nc.psum_tensor(n, s, d))

    xnT = sb("xnT", [128, KC * T], BF16)
    gate_sb = sb("gate_sb", [128, KC * T], BF16)
    yfull = sb("yfull", [128, KC * T], BF16)
    inter = sb("inter", [128, KC * T], BF16)
    acc = sb("acc", [128, KC * T], F32)
    B_sb = sb("B_sb", [128, T], F32)
    C_sb = sb("C_sb", [128, T], F32)
    d04 = sb("d04", [128, 4 * T], F32)
    a14 = sb("a14", [128, 4 * T], F32)
    dl4 = sb("dl4", [128, 4 * T], F32)
    dbbc4 = sb("dbbc4", [128, 4 * T], F32)
    db_sb = dbbc4[0:H, 0:T]
    sc_mh4 = sb("sc_mh4", [128, 4 * T], BF16)
    hcarry = sb("hcarry", [128, H], F32)
    ybuild = sb("ybuild", [128, T], F32)
    mr_sb = sb("mr_sb", [128, 2 * NT], F32)
    st1 = sb("st1", [128, 1], F32)
    st2 = sb("st2", [128, 1], F32)
    st3 = sb("st3", [128, 1], F32)
    wxw = sb("wxw", [128, KC * 288], BF16)
    wgo = sb("wgo", [128, 2 * 2048], BF16)
    gpan = sb("gpan", [128, 2 * 4096], BF16)
    upan = sb("upan", [128, 2 * 4096], BF16)
    wdp = sb("wdp", [128, 2 * 2048], BF16)
    gf1 = sb("gf1", [128, T], F32)
    gf2 = sb("gf2", [128, T], F32)
    gf3 = sb("gf3", [128, T], F32)
    gf4 = sb("gf4", [128, T], F32)
    ysb0 = sb("ysb0", [32, T], F32)
    ysb1 = sb("ysb1", [32, T], F32)
    ysb = [ysb0[:], ysb1[:]]
    sc_t1 = dl4[:, 0:T]
    sc_t2 = dl4[:, T:2 * T]
    gplist = [gpan[:, 0:4096], gpan[:, 4096:8192],
              d04[:].bitcast(BF16), dl4[:].bitcast(BF16)]
    uplist = [upan[:, 0:4096], upan[:, 4096:8192],
              a14[:].bitcast(BF16), dbbc4[:].bitcast(BF16)]
    bg_sb = sb("bg_sb", [128, KC], F32)
    dp_sb = sb("dp_sb", [128, KC], F32)
    a_sb = sb("a_sb", [128, H], F32)
    up_i32 = upan[:].bitcast(I32)
    idf_sb = up_i32[:, 3584:3712].bitcast(F32)
    idb_sb = sb("idb_sb", [128, 128], BF16)
    one_sb = sb("one_sb", [128, 32], BF16)
    c22 = sb("c22", [128, 1], I32)

    pk_i32 = gpan[:].bitcast(I32)
    acc_i32h = acc[:, 4096:8192].bitcast(I32)
    p2t_i32 = up_i32[:, 0:1024]
    dq_sc2 = up_i32[:, 1024:2048].bitcast(F32)
    dq_t1 = acc_i32h[:, 0:1024]
    dq_m2 = acc_i32h[:, 1024:2048]
    dq_mn = acc_i32h[:, 2048:3072]
    dq_sg = acc_i32h[:, 3072:4096]
    dq_bits = up_i32[:, 2048:3072]
    sqv = acc[:, 0:2048]
    inter_f32 = inter[:].bitcast(F32)
    xld = [inter_f32[:, 0:2048], inter_f32[:, 2048:4096]]
    gpan_f32 = gpan[:].bitcast(F32)
    xld_s = [gpan_f32[:, 0:2048], gpan_f32[:, 2048:4096]]
    xn_tm2 = [acc[:, 2048:3072].bitcast(BF16),
              acc[:, 3072:4096].bitcast(BF16)]
    yf_f32 = yfull[:].bitcast(F32)
    out_stage = [yf_f32[:, 0:2048], yf_f32[:, 2048:4096]]

    P = [ps_(f"P{i}", [128, 512], F32) for i in range(6)]
    PBa = ps_("PBa", [128, 512], F32)
    PBb = ps_("PBb", [128, 512], F32)
    slots = [Slot(p[:]) for p in P]
    slotA = Slot(PBa[:])
    slotB = Slot(PBb[:])
    slotA.b16 = PBa[:].bitcast(BF16)
    slotB.b16 = PBb[:].bitcast(BF16)
    ffn_pool = [slots[0], slots[1], slots[2], slots[5]]
    ffn_rot = [0]

    def next_bank():
        sl = ffn_pool[ffn_rot[0] % 4]
        ffn_rot[0] += 1
        return sl

    sem_names = (["CONST", "PK", "SCL", "GLD", "WCV", "WCV2", "TEN", "VEC", "ACT", "CC",
                  "DQS", "DQS2", "HC", "DBD", "YB",
                  "WGO0", "WGO1", "GP0", "GP1", "GP2", "GP3", "UP0", "UP1", "UP2", "UP3", "WD0", "WD1", "WD2",
                  "DBB0", "DBB1", "XLD0", "XLD1", "OUT0", "OUT1"]
                 + ["bar_" + e for e in ENGINES])
    sem_handles = {nm: st.enter_context(nc.semaphore(nm)) for nm in sem_names}
    ctx = Ctx(sem_handles)

    guards = {}

    def emit_consts(ctx):
        ctx.dma("sync", bg_sb[:], bg_in, "CONST")
        ctx.dma("sync", dp_sb[:], dp_in, "CONST")
        ctx.dma("sync", a_sb[:], a_in, "CONST")
        ctx.dma("sync", idf_sb[:], idn_f, "CONST")
        ctx.dma("sync", idb_sb[:], idn_b, "CONST")
        ctx.dma("sync", one_sb[:], ones_b, "CONST")
        ctx.dma("sync", c22[:], c22_in, "CONST")
        src = wx_p.rearrange("(c p) m -> p c m", p=128)
        dstv = wxw[:].rearrange("q (c m) -> q c m", m=288)
        ctx.dma("gpsimd", dstv, src, "GLD")
        ctx.dma("gpsimd", wgo_dram.ap()[0:DM, :], wgf_in, "WCV")
        ctx.dma("gpsimd", wgo_dram.ap()[DM:2 * DM, :], wof_in, "WCV2")
        return ctx.count["CONST"], ctx.count["GLD"]

    def emit_dequant(ctx, consts_ld, consts_gld):
        wout = [wdp[:, 0:1024], wdp[:, 1024:2048]]
        mats = [(gpk, gsc, cc1_in, 0, False), (upk, usc, cc1_in, 1024, False),
                (dpk, dsc, cc2_in, 0, True)]
        tr_rot = [0]
        for (mi, (pkt, sct, cc_t, row_base, is_down)) in enumerate(mats):
            for nq in range(4):
                for nt in range(4):
                    dst = pk_i32[:, nt * 1024:(nt + 1) * 1024]
                    if is_down:
                        srcr = pkt[(nq * 4 + nt) * 128:
                                   (nq * 4 + nt + 1) * 128, :]
                    else:
                        b0 = nq * 512 + (nt // 2) * 256
                        srcr = pkt[b0:b0 + 256, :].rearrange(
                            "(i two) k -> two i k", two=2)[nt % 2]
                    ctx.dma("sync", dst, srcr, "PK")
                pk_total = ctx.count["PK"]
                for Jc2 in range(4):
                    ctx.phase = "dqtr"
                    Jc = 2 * Jc2
                    g = guards.get("dqsc")
                    if g is not None:
                        ctx.wait("sync", *g)
                    for h in range(2):
                        ssrc = sct[8 * (Jc + h):8 * (Jc + h) + 8,
                                   nq * 512:(nq + 1) * 512] \
                            .unsqueeze(1).broadcast_to([8, 16, 512])
                        ctx.dma("sync", dq_sc2[:, h * 512:(h + 1) * 512],
                                ssrc, "SCL")
                    sc_mark = ctx.count["SCL"]
                    for h in range(2):
                        sl = slots[2 if h == 0 else 5]
                        sl.pre_write(ctx)
                        for nt in range(4):
                            ctx.wait("tensor", "PK", pk_total)
                            ctx.wait("tensor", "CONST", consts_ld)
                            ctx.op("tensor", "matmul",
                                   semname="TEN" if nt == 3 else None,
                                   out=sl.ap[:, nt * 128:(nt + 1) * 128],
                                   lhsT=pk_i32[:, nt * 1024 + (Jc + h) * 128:
                                               nt * 1024 + (Jc + h + 1) * 128]
                                   .bitcast(F32),
                                   rhs=idf_sb[:], is_transpose=True,
                                   start=True, stop=True)
                        sl.wrote("TEN", ctx.count["TEN"])
                        sl.pre_read(ctx, "vector")
                        ctx.op("vector", "tensor_copy", semname="VEC",
                               out=p2t_i32[:, h * 512:(h + 1) * 512],
                               in_=sl.ap[:].bitcast(I32))
                        sl.read_done("VEC", ctx.count["VEC"])
                    ctx.wait("vector", "SCL", sc_mark)
                    ctx.wait("vector", "CONST", consts_ld)
                    for par in range(2):
                        if par == 1:
                            ctx.op("vector", "tensor_scalar", out=dq_t1[:],
                                   in0=p2t_i32[:], scalar1=4, scalar2=None,
                                   op0=OP.logical_shift_right)
                        src32 = p2t_i32 if par == 0 else dq_t1
                        ctx.op("vector", "tensor_scalar", out=dq_m2[:],
                               in0=src32[:], scalar1=7, scalar2=None,
                               op0=OP.bitwise_and)
                        ctx.op("vector", "tensor_scalar", out=dq_mn[:],
                               in0=dq_m2[:], scalar1=252, scalar2=None,
                               op0=OP.add)
                        ctx.op("vector", "scalar_tensor_tensor", out=dq_mn[:],
                               in0=dq_m2[:], scalar=252, in1=dq_mn[:],
                               op0=OP.mult, op1=OP.min)
                        ctx.op("vector", "tensor_scalar", out=dq_sg[:],
                               in0=src32[:], scalar1=8, scalar2=28,
                               op0=OP.bitwise_and, op1=OP.logical_shift_left)
                        ctx.op("vector", "scalar_tensor_tensor",
                               out=dq_bits[:],
                               in0=dq_mn[:], scalar=c22[:, 0:1], in1=dq_sg[:],
                               op0=OP.logical_shift_left, op1=OP.bitwise_or)
                        dsem = "DQS" if par == 0 else "DQS2"
                        queue = "sync" if par == 0 else "scalar"
                        g = guards.get(("dqw", par))
                        if g is not None:
                            ctx.wait("vector", *g)
                        ctx.op("vector", "tensor_tensor", semname="VEC",
                               out=wout[par],
                               in0=dq_bits[:].bitcast(F32), in1=dq_sc2[:],
                               op=OP.mult)
                        wmark = ctx.count["VEC"]
                        ctx.wait(queue, "VEC", wmark)
                        rows = cc_t[row_base + 2 * nq * 128:
                                    row_base + (2 * nq + 2) * 128, :]
                        for h in range(2):
                            cdeq = 2 * (Jc + h) + par
                            wsrc = wout[par][:, h * 512:(h + 1) * 512]
                            if is_down:
                                for hh in range(2):
                                    dst = rows.rearrange("(J p) c -> p J c",
                                                         p=128)[
                                        :, :, hh * 2048 + cdeq * 128:
                                        hh * 2048 + (cdeq + 1) * 128]
                                    srcv = wsrc.rearrange(
                                        "p (J h c) -> p J h c", J=2, h=2)[
                                        :, :, hh, :]
                                    dqm = ctx.dma(queue, dst, srcv, dsem)
                            else:
                                dst = rows.rearrange("(J p) c -> p J c",
                                                     p=128)[
                                    :, :, cdeq * 256:(cdeq + 1) * 256]
                                srcv = wsrc.rearrange(
                                    "p (J c) -> p J c", J=2)
                                dqm = ctx.dma(queue, dst, srcv, dsem)
                        guards[("dqw", par)] = (dsem, dqm)
                    guards["dqsc"] = ("VEC", ctx.count["VEC"])
                ctx.wait("sync", "TEN", ctx.count["TEN"])
            if mi == 1:
                ctx.wait("gpsimd", "DQS", ctx.count["DQS"])
                ctx.wait("gpsimd", "DQS2", ctx.count["DQS2"])
                ctx.wait("gpsimd", "GLD", consts_gld)
                ctx.op("gpsimd", "collective_compute", semname="CC", inck=1,
                       kind="AllGather", op=OP.bypass,
                       replica_groups=[list(range(NCORES))],
                       ins=[cc1_in.ap()], outs=[cc1_out.ap()])
        ctx.wait("gpsimd", "DQS", ctx.count["DQS"])
        ctx.wait("gpsimd", "DQS2", ctx.count["DQS2"])
        ctx.op("gpsimd", "collective_compute", semname="CC", inck=1,
               kind="AllGather", op=OP.bypass,
               replica_groups=[list(range(NCORES))],
               ins=[cc2_in.ap()], outs=[cc2_out.ap()])

    def emit_stats_xn(ctx, p, consts_ld):
        for tt in range(NT):
            buf = tt % 2
            row0 = p * T + tt * 128
            g = guards.get("units_done")
            if g is not None:
                ctx.wait("sync", *g)
            else:
                ctx.wait("sync", "TEN", ctx.count["TEN"])
            g = guards.get(("xld", buf))
            if g is not None:
                ctx.wait("sync", *g)
            ctx.dma("sync", xld_s[buf], x_in[row0:row0 + 128, :],
                    "XLD%d" % buf)
            ctx.wait("vector", "XLD%d" % buf, ctx.count["XLD%d" % buf])
            ctx.wait("vector", "ACT", ctx.count["ACT"])
            ctx.op("vector", "tensor_tensor", out=sqv[:], in0=xld_s[buf],
                   in1=xld_s[buf], op=OP.mult)
            ctx.op("vector", "reduce_sum", out=st1[:], in_=xld_s[buf],
                   axis=AX.X)
            ctx.op("vector", "reduce_sum", out=st2[:], in_=sqv[:], axis=AX.X)
            ctx.op("vector", "tensor_scalar", out=mr_sb[:, 2 * tt:2 * tt + 1],
                   in0=st1[:], scalar1=1.0 / DM, scalar2=None, op0=OP.mult)
            ctx.op("vector", "tensor_scalar", out=st2[:], in0=st2[:],
                   scalar1=1.0 / DM, scalar2=None, op0=OP.mult)
            ctx.op("vector", "tensor_tensor", out=st3[:],
                   in0=mr_sb[:, 2 * tt:2 * tt + 1],
                   in1=mr_sb[:, 2 * tt:2 * tt + 1], op=OP.mult)
            ctx.op("vector", "tensor_tensor", out=st2[:], in0=st2[:], in1=st3[:],
                   op=OP.subtract)
            ctx.op("vector", "tensor_scalar", out=st2[:], in0=st2[:],
                   scalar1=1e-5, scalar2=None, op0=OP.add, semname="VEC")
            ctx.wait("scalar", "VEC", ctx.count["VEC"])
            ctx.op("scalar", "activation", out=st3[:], in_=st2[:], func=AF.Ln)
            ctx.op("scalar", "activation", out=mr_sb[:, 2 * tt + 1:2 * tt + 2],
                   in_=st3[:], func=AF.Exp, scale=-0.5, semname="ACT")
            ctx.wait("vector", "ACT", ctx.count["ACT"])
            xn_tm = xn_tm2[tt % 2]
            g = guards.get(("xn_tm", tt % 2))
            if g is not None:
                ctx.wait("vector", *g)
            ctx.op("vector", "tensor_scalar", out=xn_tm[:], in0=xld_s[buf],
                   scalar1=mr_sb[:, 2 * tt:2 * tt + 1],
                   scalar2=mr_sb[:, 2 * tt + 1:2 * tt + 2],
                   op0=OP.subtract, op1=OP.mult, semname="VEC")
            xnm = ctx.count["VEC"]
            guards[("xld", buf)] = ("VEC", xnm)
            ctx.wait("tensor", "VEC", xnm)
            ctx.wait("tensor", "CONST", consts_ld)
            ctx.phase = "stats%d" % tt
            for c in range(KC):
                J, par = c // 2, c % 2
                sl = slotA if c % 2 == 0 else slotB
                off = c // 2
                if off == 0:
                    sl.pre_write(ctx)
                src_ap = xn_tm.rearrange("q (j two) -> q j two", two=2)[
                    :, 128 * J:128 * (J + 1), par:par + 1].squeeze(-1)
                last = (c >= KC - 2)
                ctx.op("tensor", "matmul", semname="TEN" if last else None,
                       out=sl.b16[:, off * 128:(off + 1) * 128],
                       lhsT=src_ap, rhs=idb_sb[:], is_transpose=True,
                       start=True, stop=True)
                if last:
                    sl.wrote("TEN", ctx.count["TEN"])
            guards[("xn_tm", tt % 2)] = ("TEN", ctx.count["TEN"])
            slotA.pre_read(ctx, "scalar")
            slotB.pre_read(ctx, "scalar")
            for c in range(KC):
                sl = slotA if c % 2 == 0 else slotB
                off = c // 2
                ctx.op("scalar", "copy",
                       semname="ACT" if c == KC - 1 else None,
                       out=xnT[:, c * T + tt * 128: c * T + (tt + 1) * 128],
                       in_=sl.b16[:, off * 128:(off + 1) * 128])
            slotA.read_done("ACT", ctx.count["ACT"])
            slotB.read_done("ACT", ctx.count["ACT"])

    def dma_wgo(ctx, which, oc, buf, consts_gld, qw):
        ctx.wait(qw, "WCV" if which == 0 else "WCV2", 16)
        g = guards.get(("wgo", buf))
        if g is not None:
            ctx.wait(qw, *g)
        sem = "WGO%d" % buf
        r0 = which * DM + oc * 128
        ctx.dma(qw, wgo[:, buf * 2048:(buf + 1) * 2048],
                wgo_dram.ap()[r0:r0 + 128, :], sem)
        return (sem, ctx.count[sem])

    def emit_merged(ctx, p, consts_ld, consts_gld):
        xn_done = ctx.count["ACT"]
        ctx.wait("tensor", "ACT", xn_done)
        ctx.wait("tensor", "GLD", consts_gld)
        q_scan = "scalar" if p == 0 else "gpsimd"

        ctx.phase = "ssm"
        trips = ((0, 128, slots[2]), (128, 128, slots[5]), (256, 32, slotB))
        for (col0, ncols, sl) in trips:
            sl.pre_write(ctx)
            for kc in range(KC):
                last = kc == KC - 1
                ctx.op("tensor", "matmul", semname="TEN" if last else None,
                       out=sl.ap[0:ncols, :],
                       lhsT=wxw[:, kc * 288 + col0: kc * 288 + col0 + ncols],
                       rhs=xnT[:, kc * T:(kc + 1) * T],
                       start=(kc == 0), stop=last)
            sl.wrote("TEN", ctx.count["TEN"])
        for ((col0, ncols, sl), dst) in zip(trips,
                                            (B_sb[:], C_sb[:], db_sb[:])):
            sl.pre_read(ctx, "vector")
            ctx.op("vector", "tensor_copy", semname="VEC", out=dst,
                   in_=sl.ap[0:ncols, :])
            sl.read_done("VEC", ctx.count["VEC"])
        db_evac = ctx.count["VEC"]
        ctx.wait(q_scan, "VEC", db_evac)
        ctx.wait(q_scan, "DBB0", ctx.count["DBB0"])
        ctx.dma(q_scan, db_dram.ap(), db_sb[:], "DBD")
        ctx.wait(q_scan, "DBD", ctx.count["DBD"])
        for q in range(4):
            ctx.dma(
                q_scan, dbbc4[:, q * T:(q + 1) * T],
                db_dram.ap()[q:q + 1, :].partition_broadcast(128).squeeze(1),
                "DBB0")
        ctx.wait("vector", "HC", p)

        st = {"group": 0, "yg": 0, "wg": 0, "wo": 0, "slot": 0, "yf": None,
              "ymm_done": None, "mh_mark": None,
              "dbbc_mark": ctx.count["DBB0"]}

        def emit_wgo_group(which, oc):
            ctx.phase = "w%dg%d" % (which, oc)
            buf = oc % 2
            gm = dma_wgo(ctx, which, oc, buf, consts_gld, "sync")
            sl = slots[3 + buf]
            sl.pre_write(ctx)
            if which == 1 and oc == 0:
                ctx.wait("tensor", "VEC", st["yf"])
            ctx.wait("tensor", *gm)
            rhs_t = xnT if which == 0 else yfull
            for kc in range(KC):
                last = kc == KC - 1
                ctx.op("tensor", "matmul", semname="TEN" if last else None,
                       out=sl.ap[:],
                       lhsT=wgo[:, buf * 2048 + kc * 128:
                                buf * 2048 + (kc + 1) * 128],
                       rhs=rhs_t[:, kc * T:(kc + 1) * T],
                       start=(kc == 0), stop=last)
            sl.wrote("TEN", ctx.count["TEN"])
            guards[("wgo", buf)] = ("TEN", ctx.count["TEN"])
            sl.pre_read(ctx, "scalar")
            if which == 0:
                ctx.wait("scalar", "CONST", consts_ld)
                ctx.op("scalar", "activation", semname="ACT",
                       out=gate_sb[:, oc * T:(oc + 1) * T], in_=sl.ap[:],
                       func=AF.Sigmoid, bias=bg_sb[:, oc:oc + 1])
            else:
                ctx.op("scalar", "copy", semname="ACT",
                       out=gate_sb[:, oc * T:(oc + 1) * T], in_=sl.ap[:])
            sl.read_done("ACT", ctx.count["ACT"])

        def emit_scan_group():
            g4 = st["group"]
            st["group"] += 1
            h0 = 4 * g4
            ctx.wait("vector", "DBB0", st["dbbc_mark"])
            for q in range(4):
                ctx.op("vector", "tensor_tensor",
                       out=d04[:, q * T:(q + 1) * T], in0=B_sb[:],
                       in1=dbbc4[:, q * T:(q + 1) * T], op=OP.add,
                       semname="VEC" if q == 3 else None)
            d0m = ctx.count["VEC"]
            if h0 + 4 < H:
                ctx.wait(q_scan, "VEC", d0m)
                for q in range(4):
                    nh = h0 + 4 + q
                    ctx.dma(
                        q_scan, dbbc4[:, q * T:(q + 1) * T],
                        db_dram.ap()[nh:nh + 1, :].partition_broadcast(128)
                        .squeeze(1), "DBB0")
                st["dbbc_mark"] = ctx.count["DBB0"]
            ctx.wait("scalar", "VEC", d0m)
            ctx.op("scalar", "activation", out=a14[:], in_=d04[:], func=AF.Abs)
            ctx.op("scalar", "activation", out=a14[:], in_=a14[:],
                   func=AF.Exp, scale=-1.0)
            ctx.op("scalar", "activation", out=a14[:], in_=a14[:],
                   func=AF.Ln, bias=1.0, semname="ACT")
            lnm = ctx.count["ACT"]
            ctx.op("vector", "tensor_scalar", out=dl4[:], in0=d04[:],
                   scalar1=0.0, scalar2=None, op0=OP.max)
            ctx.wait("vector", "ACT", lnm)
            ctx.op("vector", "tensor_tensor", out=dl4[:], in0=dl4[:],
                   in1=a14[:], op=OP.add, semname="VEC")
            dlm = ctx.count["VEC"]
            ctx.wait("scalar", "VEC", dlm)
            ctx.wait("scalar", "CONST", consts_ld)
            for q in range(4):
                ctx.op("scalar", "activation", out=d04[:, q * T:(q + 1) * T],
                       in_=dl4[:, q * T:(q + 1) * T], func=AF.Exp,
                       scale=a_sb[:, h0 + q:h0 + q + 1],
                       semname="ACT" if q == 3 else None)
            expm = ctx.count["ACT"]
            ctx.op("vector", "tensor_tensor", out=a14[:], in0=dl4[:],
                   in1=dl4[:], op=OP.mult)
            ctx.wait("vector", "ACT", expm)
            for q in range(4):
                h = h0 + q
                init = 0.0 if p == 0 else hcarry[:, h:h + 1]
                ctx.op("vector", "tensor_tensor_scan",
                       out=dl4[:, q * T:(q + 1) * T],
                       data0=d04[:, q * T:(q + 1) * T],
                       data1=a14[:, q * T:(q + 1) * T],
                       initial=init, op0=OP.mult, op1=OP.add)
                g = guards.get(("mh", q))
                if g is not None:
                    ctx.wait("vector", *g)
                ctx.op("vector", "tensor_tensor",
                       out=sc_mh4[:, q * T:(q + 1) * T],
                       in0=dl4[:, q * T:(q + 1) * T],
                       in1=C_sb[:], op=OP.mult,
                       semname="VEC" if q == 3 else None)
                ctx.op("vector", "tensor_copy", out=hcarry[:, h:h + 1],
                       in_=dl4[:, q * T + T - 1:q * T + T])
            st["mh_mark"] = ctx.count["VEC"]

        def emit_y_group():
            ctx.phase = "y%d" % st["yg"]
            g4 = st["yg"]
            st["yg"] += 1
            if g4 == 0:
                ctx.wait("vector", "OUT0", ctx.count["OUT0"])
                ctx.wait("vector", "OUT1", ctx.count["OUT1"])
            ctx.wait("tensor", "VEC", st["mh_mark"])
            for q in range(4):
                mh_ap = sc_mh4[:, q * T:(q + 1) * T]
                sl = slotA if q % 2 == 0 else slotB
                sl.pre_write(ctx)
                ctx.op("tensor", "matmul", semname="TEN",
                       out=sl.ap[0:32, :], lhsT=one_sb[:, 0:32], rhs=mh_ap,
                       start=True, stop=True)
                guards[("mh", q)] = ("TEN", ctx.count["TEN"])
                sl.wrote("TEN", ctx.count["TEN"])
                sl.pre_read(ctx, "scalar")
                ctx.wait("scalar", "YB", ctx.count["YB"])
                ctx.op("scalar", "copy", semname="ACT",
                       out=ysb[q % 2], in_=sl.ap[0:32, :])
                sl.read_done("ACT", ctx.count["ACT"])
                if q == 0:
                    g = guards.get("ybuild")
                    if g is not None:
                        ctx.wait(q_scan, *g)
                ctx.wait(q_scan, "ACT", ctx.count["ACT"])
                ctx.dma(q_scan, ybuild[32 * q:32 * (q + 1), :], ysb[q % 2],
                        "YB")
            ctx.wait("vector", "YB", ctx.count["YB"])
            for c in (2 * g4, 2 * g4 + 1):
                ctx.op("vector", "tensor_scalar", out=sc_t1[:],
                       in0=xnT[:, c * T:(c + 1) * T],
                       scalar1=dp_sb[:, c:c + 1], scalar2=None,
                       op0=OP.mult)
                ctx.op("vector", "tensor_tensor", out=sc_t2[:],
                       in0=sc_t1[:], in1=ybuild[:], op=OP.add)
                ctx.op("vector", "tensor_tensor", semname="VEC",
                       out=yfull[:, c * T:(c + 1) * T],
                       in0=sc_t2[:], in1=gate_sb[:, c * T:(c + 1) * T],
                       op=OP.mult)
            guards["ybuild"] = ("VEC", ctx.count["VEC"])
            if g4 == H // 4 - 1:
                ctx.op("vector", "nop", semname="HC")
                st["yf"] = ctx.count["VEC"]
                st["ymm_done"] = ctx.count["TEN"]

        def aux():
            s = st["slot"]
            st["slot"] += 1
            if st["wg"] < KC and s % 2 == 1:
                emit_wgo_group(0, st["wg"])
                st["wg"] += 1
            if s % 4 == 3 and st["yg"] < st["group"]:
                emit_y_group()
            if s % 4 == 3 and st["group"] < H // 4:
                emit_scan_group()
            if st["yg"] == H // 4 and st["wo"] < KC and s % 2 == 0:
                emit_wgo_group(1, st["wo"])
                st["wo"] += 1

        inter_marks = {}
        for ns in range(NCORES):
            down_guard = guards.get("downTEN")
            for J2 in range(8):
                u = ns * 8 + J2
                if u < 18:
                    buf = u % 2
                else:
                    buf = (2, 3, 0, 1)[(u - 18) % 4]
                pm = {}
                for (plist, roff, nm, fam, q_) in (
                        (gplist, 0, "gp", "GP", "sync"),
                        (uplist, 1024, "up", "UP", "sync")):
                    g = guards.get((nm, buf))
                    if g is not None:
                        ctx.wait(q_, *g)
                    if nm == "gp" and buf < 2:
                        g = guards.get(("xld", buf))
                        if g is not None:
                            ctx.wait(q_, *g)
                    if u in (18, 19):
                        ctx.wait(q_, "VEC", st["yf"])
                    ctx.wait(q_, "CC", 1)
                    r0 = ns * 2048 + roff + J2 * 128
                    src = cc1_out.ap()[r0:r0 + 128, :]
                    dst = plist[buf]
                    sem = fam + str(buf)
                    ctx.dma(q_, dst, src, sem)
                    pm[nm] = (sem, ctx.count[sem])
                for par in range(2):
                    ci = J2 * 2 + par
                    ctx.phase = "u%d.%d.%d" % (ns, J2, par)
                    subslots = []
                    for (plist, sid, nm) in ((gplist, 0, "gp"),
                                             (uplist, 1, "up")):
                        sl = next_bank()
                        subslots.append(sl)
                        sl.pre_write(ctx)
                        ctx.wait("tensor", *pm[nm])
                        pan_ap = plist[buf]
                        for kc in range(KC):
                            last = kc == KC - 1
                            c0 = kc * 256 + par * 128
                            lhs = pan_ap[:, c0:c0 + 128]
                            ctx.op("tensor", "matmul",
                                   semname="TEN" if last else None,
                                   out=sl.ap[:], lhsT=lhs,
                                   rhs=xnT[:, kc * T:(kc + 1) * T],
                                   start=(kc == 0), stop=last)
                        sl.wrote("TEN", ctx.count["TEN"])
                    guards[("gp", buf)] = ("TEN", ctx.count["TEN"])
                    guards[("up", buf)] = ("TEN", ctx.count["TEN"])
                    slg, slu = subslots
                    slg.pre_read(ctx, "scalar")
                    slu.pre_read(ctx, "scalar")
                    g = guards.get("gelu")
                    if g is not None:
                        ctx.wait("scalar", *g)
                    ctx.op("scalar", "copy", out=gf1[:], in_=slg.ap[:])
                    ctx.op("scalar", "activation", out=gf2[:], in_=slg.ap[:],
                           func=AF.Square)
                    ctx.op("scalar", "activation", out=gf3[:], in_=slu.ap[:],
                           func=AF.Copy, scale=0.5, semname="ACT")
                    gm = ctx.count["ACT"]
                    slg.read_done("ACT", gm)
                    slu.read_done("ACT", gm)
                    ctx.wait("vector", "ACT", gm)
                    ctx.op("vector", "tensor_scalar", out=gf2[:], in0=gf2[:],
                           scalar1=GELU_C1, scalar2=GELU_C0,
                           op0=OP.mult, op1=OP.add)
                    ctx.op("vector", "tensor_tensor", out=gf2[:], in0=gf1[:],
                           in1=gf2[:], op=OP.mult, semname="VEC")
                    ctx.wait("scalar", "VEC", ctx.count["VEC"])
                    ctx.op("scalar", "activation", out=gf2[:], in_=gf2[:],
                           func=AF.Tanh, semname="ACT")
                    ctx.wait("vector", "ACT", ctx.count["ACT"])
                    ctx.op("vector", "tensor_scalar", out=gf2[:], in0=gf2[:],
                           scalar1=1.0, scalar2=None, op0=OP.add)
                    ctx.op("vector", "tensor_tensor", out=gf4[:], in0=gf1[:],
                           in1=gf3[:], op=OP.mult)
                    if down_guard is not None and J2 == 0 and par == 0:
                        ctx.wait("vector", *down_guard)
                    ctx.op("vector", "tensor_tensor", semname="VEC",
                           out=inter[:, ci * T:(ci + 1) * T],
                           in0=gf4[:], in1=gf2[:], op=OP.mult)
                    inter_marks[ci] = ctx.count["VEC"]
                    guards["gelu"] = ("VEC", ctx.count["VEC"])
                    aux()
                if ns == NCORES - 1 and J2 == 7:
                    guards["units_done"] = ("TEN", ctx.count["TEN"])
            wdlist = [wdp[:, 0:2048], wdp[:, 2048:4096], sc_mh4[:]]
            for ht in range(KC):
                buf = ht % 2 if ns == 0 else (ns * KC + ht) % 3
                ctx.phase = "dn%d.%d" % (ns, ht)
                g = guards.get(("wd", buf))
                if g is not None:
                    ctx.wait("gpsimd", *g)
                if buf == 2:
                    ctx.wait("gpsimd", "TEN", st["ymm_done"])
                ctx.wait("gpsimd", "CC", 2)
                r0 = ns * 1024 + (ht // 2) * 128
                src = cc2_out.ap()[r0:r0 + 128,
                                  (ht % 2) * 2048:(ht % 2 + 1) * 2048]
                dst = wdlist[buf]
                sem = "WD%d" % buf
                ctx.dma("gpsimd", dst, src, sem)
                sl = next_bank()
                sl.pre_write(ctx)
                ctx.wait("tensor", sem, ctx.count[sem])
                for cn in range(KC):
                    last = cn == KC - 1
                    if ht == 0:
                        ctx.wait("tensor", "VEC", inter_marks[cn])
                    ctx.op("tensor", "matmul", semname="TEN" if last else None,
                           out=sl.ap[:],
                           lhsT=wdlist[buf][:, cn * 128:(cn + 1) * 128],
                           rhs=inter[:, cn * T:(cn + 1) * T],
                           start=(cn == 0), stop=last)
                sl.wrote("TEN", ctx.count["TEN"])
                guards[("wd", buf)] = ("TEN", ctx.count["TEN"])
                if ns == 0:
                    sl.pre_read(ctx, "scalar")
                    ctx.op("scalar", "copy", semname="ACT",
                           out=acc[:, ht * T:(ht + 1) * T], in_=sl.ap[:])
                    sl.read_done("ACT", ctx.count["ACT"])
                else:
                    sl.pre_read(ctx, "vector")
                    ctx.op("vector", "tensor_tensor", semname="VEC",
                           out=acc[:, ht * T:(ht + 1) * T],
                           in0=acc[:, ht * T:(ht + 1) * T], in1=sl.ap[:],
                           op=OP.add)
                    sl.read_done("VEC", ctx.count["VEC"])
                aux()
            guards["downTEN"] = ("TEN", ctx.count["TEN"])
        assert st["group"] == H // 4 and st["yg"] == H // 4 \
            and st["wg"] == KC and st["wo"] == KC, st

    def emit_combine(ctx, p):
        acc_v = ctx.count["VEC"]
        acc_a = ctx.count["ACT"]
        for k4 in range(KC // 4):
            c0 = k4 * 4 * T
            ctx.wait("scalar", "VEC", acc_v)
            ctx.wait("scalar", "TEN", ctx.count["TEN"])
            g = guards.get("sgf")
            if g is not None:
                ctx.wait("scalar", *g)
            ctx.op("scalar", "activation", semname="ACT", out=d04[:],
                   in_=acc[:, c0:c0 + 4 * T], func=AF.Sigmoid)
            ctx.wait("vector", "ACT", ctx.count["ACT"])
            ctx.op("vector", "tensor_tensor", out=a14[:],
                   in0=acc[:, c0:c0 + 4 * T], in1=d04[:], op=OP.mult)
            ctx.op("vector", "tensor_tensor", semname="VEC",
                   out=gate_sb[:, c0:c0 + 4 * T],
                   in0=a14[:], in1=gate_sb[:, c0:c0 + 4 * T],
                   op=OP.mult)
            guards["sgf"] = ("VEC", ctx.count["VEC"])
        sg_done = ctx.count["VEC"]
        ctx.wait("tensor", "VEC", sg_done)
        xld_c = [dl4[:], dbbc4[:]]
        for tt in range(NT):
            buf = tt % 2
            row0 = p * T + tt * 128
            g = guards.get(("xldc", buf))
            if g is not None:
                ctx.wait("sync", *g)
            g = guards.get("units_done")
            if g is not None:
                ctx.wait("sync", *g)
            ctx.dma("sync", xld_c[buf], x_in[row0:row0 + 128, :],
                    "XLD%d" % buf)
            g = guards.get(("ostage", buf))
            if g is not None:
                ctx.wait("vector", *g)
            for ht in range(KC):
                sl = slotA if ht % 2 == 0 else slotB
                ctx.phase = "comb%d.%d" % (tt, ht)
                off = ht // 2
                sl.pre_write(ctx)
                ctx.op("tensor", "matmul", semname="TEN",
                       out=sl.b16[:, off * 128:(off + 1) * 128],
                       lhsT=gate_sb[:, ht * T + tt * 128: ht * T + (tt + 1) * 128],
                       rhs=idb_sb[:], is_transpose=True, start=True, stop=True)
                sl.wrote("TEN", ctx.count["TEN"])
                sl.pre_read(ctx, "vector")
                ctx.wait("vector", "XLD%d" % buf, ctx.count["XLD%d" % buf])
                ctx.op("vector", "tensor_tensor", semname="VEC",
                       out=out_stage[buf][:, ht * 128:(ht + 1) * 128],
                       in0=xld_c[buf][:, ht * 128:(ht + 1) * 128],
                       in1=sl.b16[:, off * 128:(off + 1) * 128], op=OP.add)
                sl.read_done("VEC", ctx.count["VEC"])
            vm = ctx.count["VEC"]
            ctx.wait("sync", "VEC", vm)
            sem = "OUT%d" % buf
            ctx.dma("sync", out_t[row0:row0 + 128, :], out_stage[buf], sem)
            guards[("xldc", buf)] = (sem, ctx.count[sem])
            guards[("ostage", buf)] = (sem, ctx.count[sem])

    def schedule(ctx):
        for sl in slots + [slotA, slotB]:
            sl.reset()
        guards.clear()
        ffn_rot[0] = 0
        consts_ld, consts_gld = emit_consts(ctx)
        emit_dequant(ctx, consts_ld, consts_gld)
        for p in range(NPASS):
            emit_stats_xn(ctx, p, consts_ld)
            emit_merged(ctx, p, consts_ld, consts_gld)
            emit_combine(ctx, p)
        ctx.wait("sync", "OUT0", ctx.count["OUT0"])
        ctx.wait("sync", "OUT1", ctx.count["OUT1"])

    with nc.Block() as block:
        @block.sync
        def _(eng):
            ctx.begin("sync", eng)
            schedule(ctx)

        @block.tensor
        def _(eng):
            ctx.begin("tensor", eng)
            schedule(ctx)

        @block.vector
        def _(eng):
            ctx.begin("vector", eng)
            schedule(ctx)

        @block.scalar
        def _(eng):
            ctx.begin("scalar", eng)
            schedule(ctx)

        @block.gpsimd
        def _(eng):
            ctx.begin("gpsimd", eng)
            schedule(ctx)

    st.close()
    return nc


def prep_inputs(inputs):
    perm = build_perm()
    x = np.asarray(inputs["L0_attn_x"], dtype=np.float32)
    Wx = np.asarray(inputs["L0_attn_Wx"], dtype=np.float32)
    Wg = np.asarray(inputs["L0_attn_Wg"], dtype=np.float32)
    bg = np.asarray(inputs["L0_attn_bg"], dtype=np.float32)
    Wo = np.asarray(inputs["L0_attn_Wo"], dtype=np.float32)
    A = np.asarray(inputs["L0_attn_A"], dtype=np.float32)
    Dp = np.asarray(inputs["L0_attn_Dp"], dtype=np.float32)
    gpkf = np.asarray(inputs["L0_ffn_gate_W_packed"], dtype=np.int32)
    gscales = np.asarray(inputs["L0_ffn_gate_scales"], dtype=np.float32)
    upkf = np.asarray(inputs["L0_ffn_up_W_packed"], dtype=np.int32)
    uscales = np.asarray(inputs["L0_ffn_up_scales"], dtype=np.float32)
    dpkf = np.asarray(inputs["L0_ffn_down_W_packed"], dtype=np.int32)
    dscales = np.asarray(inputs["L0_ffn_down_scales"], dtype=np.float32)

    wx_p = np.ascontiguousarray(Wx[perm])
    wg_p = np.ascontiguousarray(Wg[perm][:, perm])
    wo_p = np.ascontiguousarray(Wo[perm])
    wgf = np.ascontiguousarray(
        wg_p.reshape(KC, 128, KC, 128).transpose(2, 1, 0, 3).reshape(DM, DM))
    wof = np.ascontiguousarray(
        wo_p.reshape(KC, 128, KC, 128).transpose(2, 1, 0, 3).reshape(DM, DM))
    bg_in = np.ascontiguousarray(bg[perm].reshape(KC, 128).T)
    dp_in = np.ascontiguousarray(np.repeat(Dp, HD)[perm].reshape(KC, 128).T)
    a_in = np.ascontiguousarray(A.T)
    mlp_perm = np.arange(SLICE_N).reshape(4, 2, 128, 2) \
        .transpose(0, 1, 3, 2).reshape(-1)
    idn_f = np.eye(128, dtype=np.float32)
    idn_b = np.eye(128, dtype=np.float32).astype(ml_dtypes.bfloat16)
    ones_b = np.ones((128, 32), dtype=np.float32).astype(ml_dtypes.bfloat16)
    c22_in = np.full((128, 1), 22, dtype=np.int32)
    dscT = np.ascontiguousarray(dscales.T)

    in_maps = []
    for i in range(NCORES):
        in_maps.append({
            "x_in": np.ascontiguousarray(x[i]),
            "wx_p": wx_p,
            "bg_in": bg_in, "dp_in": dp_in, "a_in": a_in,
            "gpk": np.ascontiguousarray(gpkf[i * SLICE_N:(i + 1) * SLICE_N]),
            "upk": np.ascontiguousarray(upkf[i * SLICE_N:(i + 1) * SLICE_N]),
            "dpk": np.ascontiguousarray(dpkf[:, i * SLICE_J:(i + 1) * SLICE_J]),
            "gsc": np.ascontiguousarray(
                gscales[i * SLICE_N:(i + 1) * SLICE_N].T[:, mlp_perm]),
            "usc": np.ascontiguousarray(
                uscales[i * SLICE_N:(i + 1) * SLICE_N].T[:, mlp_perm]),
            "dsc": np.ascontiguousarray(dscT[i * 64:(i + 1) * 64]),
            "idn_f": idn_f, "idn_b": idn_b, "ones_b": ones_b, "c22_in": c22_in,
            "wgf": wgf, "wof": wof,
        })
    return in_maps


def kernel(**inputs):
    in_maps = prep_inputs(inputs)
    nc = build()
    res = run_bass_kernel_spmd(nc, in_maps, core_ids=list(range(NCORES)))
    out = np.stack([np.asarray(res.results[i]["out"]) for i in range(NCORES)],
                   axis=0)
    return out.astype(np.float32)


# version 36
# speedup vs baseline: 1.0696x; 1.0026x over previous
import numpy as np
import ml_dtypes
from contextlib import ExitStack

import concourse.bass as bass
import concourse.mybir as mybir
from concourse.bass_utils import run_bass_kernel_spmd

F32 = mybir.dt.float32
BF16 = mybir.dt.bfloat16
I32 = mybir.dt.int32
OP = mybir.AluOpType
AF = mybir.ActivationFunctionType
AX = mybir.AxisListType

NCORES = 8
B, L, DM = 8, 2048, 2048
H, N, HD = 32, 128, 64
MLP = 16384
T = 512
NPASS = L // T
KC = DM // 128
NT = T // 128
SLICE_N = MLP // NCORES
SLICE_J = SLICE_N // 2
GELU_C0 = 0.7978845608028654
GELU_C1 = 0.044715 * GELU_C0
CC_ROWS = 3072
CC_COLS = 4096
ENGINES = ("sync", "tensor", "vector", "scalar", "gpsimd")


def build_perm():
    perm = np.empty(DM, dtype=np.int64)
    for c in range(KC):
        J, par = c // 2, c % 2
        perm[c * 128:(c + 1) * 128] = 256 * J + 2 * np.arange(128) + par
    return perm


class Ctx:

    def __init__(self, sem_handles):
        self.sem_handles = sem_handles

    def begin(self, name, eng):
        self.active = name
        self.eng = eng
        self.count = {k: 0 for k in self.sem_handles}

    def inc(self, instr, semname, k=1):
        if instr is not None:
            instr.then_inc(self.sem_handles[semname], k)
        self.count[semname] += k
        return self.count[semname]

    def wait(self, engine, semname, val):
        if val is None or val <= 0:
            return
        if self.active == engine:
            self.eng.wait_ge(self.sem_handles[semname], val)

    def dma(self, queue, out, in_, semname):
        instr = None
        if self.active == queue:
            instr = self.eng.dma_start(out=out, in_=in_)
        return self.inc(instr, semname, 16)

    phase = "?"
    pe_log = None

    def op(self, engine, method, semname=None, inck=1, **kw):
        if engine == "tensor" and self.pe_log is not None \
                and self.active == "tensor":
            self.pe_log.append(self.phase)
        instr = None
        if self.active == engine:
            instr = getattr(self.eng, method)(**kw)
        if semname is not None:
            return self.inc(instr, semname, inck)
        return None

    def barrier(self):
        vals = {}
        for e in ENGINES:
            instr = None
            if self.active == e:
                instr = self.eng.nop()
            vals[e] = self.inc(instr, "bar_" + e, 1)
        for e in ENGINES:
            for o in ENGINES:
                if o != e:
                    self.wait(e, "bar_" + o, vals[o])


class Slot:

    def __init__(self, ap):
        self.ap = ap
        self.last_read = None
        self.last_write = None

    def reset(self):
        self.last_read = None
        self.last_write = None

    def pre_write(self, ctx):
        if self.last_read is not None:
            ctx.wait("tensor", *self.last_read)

    def wrote(self, semname, count):
        self.last_write = (semname, count)

    def pre_read(self, ctx, engine):
        ctx.wait(engine, *self.last_write)

    def read_done(self, semname, count):
        self.last_read = (semname, count)


def build():
    nc = bass.Bass("TRN2", target_bir_lowering=False, debug=False)

    def din(name, shape, dt):
        return nc.dram_tensor(name, shape, dt, kind="ExternalInput").ap()

    x_in = din("x_in", [L, DM], F32)
    wx_p = din("wx_p", [DM, 288], F32)
    bg_in = din("bg_in", [128, KC], F32)
    dp_in = din("dp_in", [128, KC], F32)
    a_in = din("a_in", [128, H], F32)
    gpk = din("gpk", [SLICE_N, DM // 2], I32)
    upk = din("upk", [SLICE_N, DM // 2], I32)
    dpk = din("dpk", [DM, SLICE_J], I32)
    gsc = din("gsc", [64, SLICE_N], F32)
    usc = din("usc", [64, SLICE_N], F32)
    dsc = din("dsc", [64, DM], F32)
    idn_f = din("idn_f", [128, 128], F32)
    idn_b = din("idn_b", [128, 128], BF16)
    ones_b = din("ones_b", [128, 32], BF16)
    c22_in = din("c22_in", [128, 1], I32)
    wgf_in = din("wgf", [DM, DM], F32)
    wof_in = din("wof", [DM, DM], F32)
    out_t = nc.dram_tensor("out", [L, DM], F32, kind="ExternalOutput").ap()

    cc1_in = nc.dram_tensor("cc1_in", [2048, CC_COLS], BF16)
    cc1_out = nc.dram_tensor("cc1_out", [NCORES * 2048, CC_COLS], BF16,
                             addr_space="Shared")
    cc2_in = nc.dram_tensor("cc2_in", [1024, CC_COLS], BF16)
    cc2_out = nc.dram_tensor("cc2_out", [NCORES * 1024, CC_COLS], BF16,
                             addr_space="Shared")
    wgo_dram = nc.dram_tensor("wgo_dram", [2 * DM, DM], BF16)
    db_dram = nc.dram_tensor("db_dram", [H, T], F32)

    st = ExitStack()
    st.__enter__()
    sb = lambda n, s, d: st.enter_context(nc.sbuf_tensor(n, s, d))
    ps_ = lambda n, s, d: st.enter_context(nc.psum_tensor(n, s, d))

    xnT = sb("xnT", [128, KC * T], BF16)
    gate_sb = sb("gate_sb", [128, KC * T], BF16)
    yfull = sb("yfull", [128, KC * T], BF16)
    inter = sb("inter", [128, KC * T], BF16)
    acc = sb("acc", [128, KC * T], F32)
    B_sb = sb("B_sb", [128, T], F32)
    C_sb = sb("C_sb", [128, T], F32)
    d04 = sb("d04", [128, 4 * T], F32)
    a14 = sb("a14", [128, 4 * T], F32)
    dl4 = sb("dl4", [128, 4 * T], F32)
    dbbc4 = sb("dbbc4", [128, 4 * T], F32)
    db_sb = dbbc4[0:H, 0:T]
    sc_mh4 = sb("sc_mh4", [128, 4 * T], BF16)
    hcarry = sb("hcarry", [128, H], F32)
    ybuild = sb("ybuild", [128, T], F32)
    mr_sb = sb("mr_sb", [128, 2 * NT], F32)
    st1 = sb("st1", [128, 1], F32)
    st2 = sb("st2", [128, 1], F32)
    st3 = sb("st3", [128, 1], F32)
    wxw = sb("wxw", [128, KC * 288], BF16)
    wgo = sb("wgo", [128, 2 * 2048], BF16)
    gpan = sb("gpan", [128, 2 * 4096], BF16)
    upan = sb("upan", [128, 2 * 4096], BF16)
    wdp = sb("wdp", [128, 2 * 2048], BF16)
    gf1 = sb("gf1", [128, T], F32)
    gf2 = sb("gf2", [128, T], F32)
    gf3 = sb("gf3", [128, T], F32)
    gf4 = sb("gf4", [128, T], F32)
    ysb0 = sb("ysb0", [32, T], F32)
    ysb1 = sb("ysb1", [32, T], F32)
    ysb = [ysb0[:], ysb1[:]]
    sc_t1 = dl4[:, 0:T]
    sc_t2 = dl4[:, T:2 * T]
    gplist = [gpan[:, 0:4096], gpan[:, 4096:8192],
              d04[:].bitcast(BF16), dl4[:].bitcast(BF16)]
    uplist = [upan[:, 0:4096], upan[:, 4096:8192],
              a14[:].bitcast(BF16), dbbc4[:].bitcast(BF16)]
    bg_sb = sb("bg_sb", [128, KC], F32)
    dp_sb = sb("dp_sb", [128, KC], F32)
    a_sb = sb("a_sb", [128, H], F32)
    up_i32 = upan[:].bitcast(I32)
    idf_sb = up_i32[:, 3584:3712].bitcast(F32)
    idb_sb = sb("idb_sb", [128, 128], BF16)
    one_sb = sb("one_sb", [128, 32], BF16)
    c22 = sb("c22", [128, 1], I32)

    pk_i32 = gpan[:].bitcast(I32)
    pk2_i32 = inter[:].bitcast(I32)
    acc_i32h = acc[:, 4096:8192].bitcast(I32)
    p2t_i32 = up_i32[:, 0:1024]
    dq_sc2 = up_i32[:, 1024:2048].bitcast(F32)
    dq_t1 = acc_i32h[:, 0:1024]
    dq_m2 = acc_i32h[:, 1024:2048]
    dq_mn = acc_i32h[:, 2048:3072]
    dq_sg = acc_i32h[:, 3072:4096]
    dq_bits = up_i32[:, 2048:3072]
    sqv = acc[:, 0:2048]
    inter_f32 = inter[:].bitcast(F32)
    xld = [inter_f32[:, 0:2048], inter_f32[:, 2048:4096]]
    gpan_f32 = gpan[:].bitcast(F32)
    xld_s = [gpan_f32[:, 0:2048], gpan_f32[:, 2048:4096]]
    xn_tm2 = [acc[:, 2048:3072].bitcast(BF16),
              acc[:, 3072:4096].bitcast(BF16)]
    yf_f32 = yfull[:].bitcast(F32)
    out_stage = [yf_f32[:, 0:2048], yf_f32[:, 2048:4096]]

    P = [ps_(f"P{i}", [128, 512], F32) for i in range(6)]
    PBa = ps_("PBa", [128, 512], F32)
    PBb = ps_("PBb", [128, 512], F32)
    slots = [Slot(p[:]) for p in P]
    slotA = Slot(PBa[:])
    slotB = Slot(PBb[:])
    slotA.b16 = PBa[:].bitcast(BF16)
    slotB.b16 = PBb[:].bitcast(BF16)
    ffn_pool = [slots[0], slots[1], slots[2], slots[5]]
    ffn_rot = [0]

    def next_bank():
        sl = ffn_pool[ffn_rot[0] % 4]
        ffn_rot[0] += 1
        return sl

    sem_names = (["CONST", "PK", "PK1", "SCL", "GLD", "WCV", "WCV2", "TEN", "VEC", "ACT", "CC",
                  "HC", "DBD", "YB"]
                 + ["DQ%d%d" % (c, p) for c in range(6) for p in range(2)]
                 + [
                  "WGO0", "WGO1", "GP0", "GP1", "GP2", "GP3", "UP0", "UP1", "UP2", "UP3", "WD0", "WD1", "WD2",
                  "DBB0", "DBB1", "XLD0", "XLD1", "OUT0", "OUT1"]
                 + ["bar_" + e for e in ENGINES])
    sem_handles = {nm: st.enter_context(nc.semaphore(nm)) for nm in sem_names}
    ctx = Ctx(sem_handles)

    guards = {}

    def emit_consts(ctx):
        ctx.dma("sync", bg_sb[:], bg_in, "CONST")
        ctx.dma("sync", dp_sb[:], dp_in, "CONST")
        ctx.dma("sync", a_sb[:], a_in, "CONST")
        ctx.dma("sync", idf_sb[:], idn_f, "CONST")
        ctx.dma("sync", idb_sb[:], idn_b, "CONST")
        ctx.dma("sync", one_sb[:], ones_b, "CONST")
        ctx.dma("sync", c22[:], c22_in, "CONST")
        src = wx_p.rearrange("(c p) m -> p c m", p=128)
        dstv = wxw[:].rearrange("q (c m) -> q c m", m=288)
        ctx.dma("gpsimd", dstv, src, "GLD")
        ctx.dma("gpsimd", wgo_dram.ap()[0:DM, :], wgf_in, "WCV")
        ctx.dma("gpsimd", wgo_dram.ap()[DM:2 * DM, :], wof_in, "WCV2")
        return ctx.count["CONST"], ctx.count["GLD"]

    def emit_dequant(ctx, consts_ld, consts_gld):
        wout = [wdp[:, 0:1024], wdp[:, 1024:2048]]
        mats = [(gpk, gsc, cc1_in, 0, False), (upk, usc, cc1_in, 1024, False),
                (dpk, dsc, cc2_in, 0, True)]
        steps = [(0, 0), (0, 1), (1, 0), (1, 1), (0, 2), (0, 3),
                 (1, 2), (1, 3), (2, 0), (2, 1), (2, 2), (2, 3)]
        BLK = NCORES * 512
        ags = {1: (cc1_in, 0, 512, cc1_out, 0),
               3: (cc1_in, 1024, 1536, cc1_out, 2),
               5: (cc1_in, 512, 1024, cc1_out, 1),
               7: (cc1_in, 1536, 2048, cc1_out, 3),
               9: (cc2_in, 0, 512, cc2_out, 0),
               11: (cc2_in, 512, 1024, cc2_out, 1)}
        pkbufs = [pk_i32, pk2_i32]
        pk_guard = {}
        pk_mark = {}

        def emit_pk(s):
            mi, nq = steps[s]
            pkt = mats[mi][0]
            is_down = mats[mi][4]
            buf = pkbufs[s % 2]
            sem = "PK" if s % 2 == 0 else "PK1"
            g = pk_guard.get(s % 2)
            if g is not None:
                ctx.wait("sync", "TEN", g)
            for nt in range(4):
                dst = buf[:, nt * 1024:(nt + 1) * 1024]
                if is_down:
                    srcr = pkt[(nq * 4 + nt) * 128:
                               (nq * 4 + nt + 1) * 128, :]
                else:
                    b0 = nq * 512 + (nt // 2) * 256
                    srcr = pkt[b0:b0 + 256, :].rearrange(
                        "(i two) k -> two i k", two=2)[nt % 2]
                ctx.dma("sync", dst, srcr, sem)
            pk_mark[s] = (sem, ctx.count[sem])

        emit_pk(0)
        for s, (mi, nq) in enumerate(steps):
            (pkt, sct, cc_t, row_base, is_down) = mats[mi]
            if s + 1 < len(steps):
                emit_pk(s + 1)
            cur = pkbufs[s % 2]
            pk_total = pk_mark[s]
            if True:
                for Jc2 in range(4):
                    ctx.phase = "dqtr"
                    Jc = 2 * Jc2
                    g = guards.get("dqsc")
                    if g is not None:
                        ctx.wait("scalar", *g)
                    for h in range(2):
                        ssrc = sct[8 * (Jc + h):8 * (Jc + h) + 8,
                                   nq * 512:(nq + 1) * 512] \
                            .unsqueeze(1).broadcast_to([8, 16, 512])
                        ctx.dma("scalar", dq_sc2[:, h * 512:(h + 1) * 512],
                                ssrc, "SCL")
                    sc_mark = ctx.count["SCL"]
                    for h in range(2):
                        sl = slots[2 if h == 0 else 5]
                        sl.pre_write(ctx)
                        for nt in range(4):
                            ctx.wait("tensor", *pk_total)
                            ctx.wait("tensor", "CONST", consts_ld)
                            ctx.op("tensor", "matmul",
                                   semname="TEN" if nt == 3 else None,
                                   out=sl.ap[:, nt * 128:(nt + 1) * 128],
                                   lhsT=cur[:, nt * 1024 + (Jc + h) * 128:
                                            nt * 1024 + (Jc + h + 1) * 128]
                                   .bitcast(F32),
                                   rhs=idf_sb[:], is_transpose=True,
                                   start=True, stop=True)
                        sl.wrote("TEN", ctx.count["TEN"])
                        sl.pre_read(ctx, "vector")
                        ctx.op("vector", "tensor_copy", semname="VEC",
                               out=p2t_i32[:, h * 512:(h + 1) * 512],
                               in_=sl.ap[:].bitcast(I32))
                        sl.read_done("VEC", ctx.count["VEC"])
                    ctx.wait("vector", "SCL", sc_mark)
                    ctx.wait("vector", "CONST", consts_ld)
                    for par in range(2):
                        if par == 1:
                            ctx.op("vector", "tensor_scalar", out=dq_t1[:],
                                   in0=p2t_i32[:], scalar1=4, scalar2=None,
                                   op0=OP.logical_shift_right)
                        src32 = p2t_i32 if par == 0 else dq_t1
                        ctx.op("vector", "tensor_scalar", out=dq_m2[:],
                               in0=src32[:], scalar1=7, scalar2=None,
                               op0=OP.bitwise_and)
                        ctx.op("vector", "tensor_scalar", out=dq_mn[:],
                               in0=dq_m2[:], scalar1=252, scalar2=None,
                               op0=OP.add)
                        ctx.op("vector", "scalar_tensor_tensor", out=dq_mn[:],
                               in0=dq_m2[:], scalar=252, in1=dq_mn[:],
                               op0=OP.mult, op1=OP.min)
                        ctx.op("vector", "tensor_scalar", out=dq_sg[:],
                               in0=src32[:], scalar1=8, scalar2=28,
                               op0=OP.bitwise_and, op1=OP.logical_shift_left)
                        ctx.op("vector", "scalar_tensor_tensor",
                               out=dq_bits[:],
                               in0=dq_mn[:], scalar=c22[:, 0:1], in1=dq_sg[:],
                               op0=OP.logical_shift_left, op1=OP.bitwise_or)
                        dsem = "DQ%d%d" % (s // 2, par)
                        queue = "sync" if par == 0 else "scalar"
                        g = guards.get(("dqw", par))
                        if g is not None:
                            ctx.wait("vector", *g)
                        ctx.op("vector", "tensor_tensor", semname="VEC",
                               out=wout[par],
                               in0=dq_bits[:].bitcast(F32), in1=dq_sc2[:],
                               op=OP.mult)
                        wmark = ctx.count["VEC"]
                        ctx.wait(queue, "VEC", wmark)
                        rows = cc_t[row_base + 2 * nq * 128:
                                    row_base + (2 * nq + 2) * 128, :]
                        for h in range(2):
                            cdeq = 2 * (Jc + h) + par
                            wsrc = wout[par][:, h * 512:(h + 1) * 512]
                            if is_down:
                                for hh in range(2):
                                    dst = rows.rearrange("(J p) c -> p J c",
                                                         p=128)[
                                        :, :, hh * 2048 + cdeq * 128:
                                        hh * 2048 + (cdeq + 1) * 128]
                                    srcv = wsrc.rearrange(
                                        "p (J h c) -> p J h c", J=2, h=2)[
                                        :, :, hh, :]
                                    dqm = ctx.dma(queue, dst, srcv, dsem)
                            else:
                                dst = rows.rearrange("(J p) c -> p J c",
                                                     p=128)[
                                    :, :, cdeq * 256:(cdeq + 1) * 256]
                                srcv = wsrc.rearrange(
                                    "p (J c) -> p J c", J=2)
                                dqm = ctx.dma(queue, dst, srcv, dsem)
                        guards[("dqw", par)] = (dsem, dqm)
                    guards["dqsc"] = ("VEC", ctx.count["VEC"])
            pk_guard[s % 2] = ctx.count["TEN"]
            if s in ags:
                (t_in, r0, r1, t_out, blk) = ags[s]
                ctx.wait("gpsimd", "DQ%d0" % (s // 2),
                         ctx.count["DQ%d0" % (s // 2)])
                ctx.wait("gpsimd", "DQ%d1" % (s // 2),
                         ctx.count["DQ%d1" % (s // 2)])
                if s == 1:
                    ctx.wait("gpsimd", "GLD", consts_gld)
                ctx.op("gpsimd", "collective_compute", semname="CC", inck=1,
                       kind="AllGather", op=OP.bypass,
                       replica_groups=[list(range(NCORES))],
                       ins=[t_in.ap()[r0:r1, :]],
                       outs=[t_out.ap()[blk * BLK:(blk + 1) * BLK, :]])

    def emit_stats_xn(ctx, p, consts_ld):
        for tt in range(NT):
            buf = tt % 2
            row0 = p * T + tt * 128
            g = guards.get("units_done")
            if g is not None:
                ctx.wait("sync", *g)
            else:
                ctx.wait("sync", "TEN", ctx.count["TEN"])
            g = guards.get(("xld", buf))
            if g is not None:
                ctx.wait("sync", *g)
            ctx.dma("sync", xld_s[buf], x_in[row0:row0 + 128, :],
                    "XLD%d" % buf)
            ctx.wait("vector", "XLD%d" % buf, ctx.count["XLD%d" % buf])
            ctx.wait("vector", "ACT", ctx.count["ACT"])
            ctx.op("vector", "tensor_tensor", out=sqv[:], in0=xld_s[buf],
                   in1=xld_s[buf], op=OP.mult)
            ctx.op("vector", "reduce_sum", out=st1[:], in_=xld_s[buf],
                   axis=AX.X)
            ctx.op("vector", "reduce_sum", out=st2[:], in_=sqv[:], axis=AX.X)
            ctx.op("vector", "tensor_scalar", out=mr_sb[:, 2 * tt:2 * tt + 1],
                   in0=st1[:], scalar1=1.0 / DM, scalar2=None, op0=OP.mult)
            ctx.op("vector", "tensor_scalar", out=st2[:], in0=st2[:],
                   scalar1=1.0 / DM, scalar2=None, op0=OP.mult)
            ctx.op("vector", "tensor_tensor", out=st3[:],
                   in0=mr_sb[:, 2 * tt:2 * tt + 1],
                   in1=mr_sb[:, 2 * tt:2 * tt + 1], op=OP.mult)
            ctx.op("vector", "tensor_tensor", out=st2[:], in0=st2[:], in1=st3[:],
                   op=OP.subtract)
            ctx.op("vector", "tensor_scalar", out=st2[:], in0=st2[:],
                   scalar1=1e-5, scalar2=None, op0=OP.add, semname="VEC")
            ctx.wait("scalar", "VEC", ctx.count["VEC"])
            ctx.op("scalar", "activation", out=st3[:], in_=st2[:], func=AF.Ln)
            ctx.op("scalar", "activation", out=mr_sb[:, 2 * tt + 1:2 * tt + 2],
                   in_=st3[:], func=AF.Exp, scale=-0.5, semname="ACT")
            ctx.wait("vector", "ACT", ctx.count["ACT"])
            xn_tm = xn_tm2[tt % 2]
            g = guards.get(("xn_tm", tt % 2))
            if g is not None:
                ctx.wait("vector", *g)
            ctx.op("vector", "tensor_scalar", out=xn_tm[:], in0=xld_s[buf],
                   scalar1=mr_sb[:, 2 * tt:2 * tt + 1],
                   scalar2=mr_sb[:, 2 * tt + 1:2 * tt + 2],
                   op0=OP.subtract, op1=OP.mult, semname="VEC")
            xnm = ctx.count["VEC"]
            guards[("xld", buf)] = ("VEC", xnm)
            ctx.wait("tensor", "VEC", xnm)
            ctx.wait("tensor", "CONST", consts_ld)
            ctx.phase = "stats%d" % tt
            for c in range(KC):
                J, par = c // 2, c % 2
                sl = slotA if c % 2 == 0 else slotB
                off = c // 2
                if off == 0:
                    sl.pre_write(ctx)
                src_ap = xn_tm.rearrange("q (j two) -> q j two", two=2)[
                    :, 128 * J:128 * (J + 1), par:par + 1].squeeze(-1)
                last = (c >= KC - 2)
                ctx.op("tensor", "matmul", semname="TEN" if last else None,
                       out=sl.b16[:, off * 128:(off + 1) * 128],
                       lhsT=src_ap, rhs=idb_sb[:], is_transpose=True,
                       start=True, stop=True)
                if last:
                    sl.wrote("TEN", ctx.count["TEN"])
            guards[("xn_tm", tt % 2)] = ("TEN", ctx.count["TEN"])
            slotA.pre_read(ctx, "scalar")
            slotB.pre_read(ctx, "scalar")
            for c in range(KC):
                sl = slotA if c % 2 == 0 else slotB
                off = c // 2
                ctx.op("scalar", "copy",
                       semname="ACT" if c == KC - 1 else None,
                       out=xnT[:, c * T + tt * 128: c * T + (tt + 1) * 128],
                       in_=sl.b16[:, off * 128:(off + 1) * 128])
            slotA.read_done("ACT", ctx.count["ACT"])
            slotB.read_done("ACT", ctx.count["ACT"])

    def dma_wgo(ctx, which, oc, buf, consts_gld, qw):
        ctx.wait(qw, "WCV" if which == 0 else "WCV2", 16)
        g = guards.get(("wgo", buf))
        if g is not None:
            ctx.wait(qw, *g)
        sem = "WGO%d" % buf
        r0 = which * DM + oc * 128
        ctx.dma(qw, wgo[:, buf * 2048:(buf + 1) * 2048],
                wgo_dram.ap()[r0:r0 + 128, :], sem)
        return (sem, ctx.count[sem])

    def emit_merged(ctx, p, consts_ld, consts_gld):
        xn_done = ctx.count["ACT"]
        ctx.wait("tensor", "ACT", xn_done)
        ctx.wait("tensor", "GLD", consts_gld)
        q_scan = "scalar" if p == 0 else "gpsimd"

        ctx.phase = "ssm"
        trips = ((0, 128, slots[2]), (128, 128, slots[5]), (256, 32, slotB))
        for (col0, ncols, sl) in trips:
            sl.pre_write(ctx)
            for kc in range(KC):
                last = kc == KC - 1
                ctx.op("tensor", "matmul", semname="TEN" if last else None,
                       out=sl.ap[0:ncols, :],
                       lhsT=wxw[:, kc * 288 + col0: kc * 288 + col0 + ncols],
                       rhs=xnT[:, kc * T:(kc + 1) * T],
                       start=(kc == 0), stop=last)
            sl.wrote("TEN", ctx.count["TEN"])
        for ((col0, ncols, sl), dst) in zip(trips,
                                            (B_sb[:], C_sb[:], db_sb[:])):
            sl.pre_read(ctx, "vector")
            ctx.op("vector", "tensor_copy", semname="VEC", out=dst,
                   in_=sl.ap[0:ncols, :])
            sl.read_done("VEC", ctx.count["VEC"])
        db_evac = ctx.count["VEC"]
        ctx.wait(q_scan, "VEC", db_evac)
        ctx.wait(q_scan, "DBB0", ctx.count["DBB0"])
        ctx.dma(q_scan, db_dram.ap(), db_sb[:], "DBD")
        ctx.wait(q_scan, "DBD", ctx.count["DBD"])
        for q in range(4):
            ctx.dma(
                q_scan, dbbc4[:, q * T:(q + 1) * T],
                db_dram.ap()[q:q + 1, :].partition_broadcast(128).squeeze(1),
                "DBB0")
        ctx.wait("vector", "HC", p)

        st = {"group": 0, "yg": 0, "wg": 0, "wo": 0, "slot": 0, "yf": None,
              "ymm_done": None, "mh_mark": None,
              "dbbc_mark": ctx.count["DBB0"]}

        def emit_wgo_group(which, oc):
            ctx.phase = "w%dg%d" % (which, oc)
            buf = oc % 2
            gm = dma_wgo(ctx, which, oc, buf, consts_gld, "sync")
            sl = slots[3 + buf]
            sl.pre_write(ctx)
            if which == 1 and oc == 0:
                ctx.wait("tensor", "VEC", st["yf"])
            ctx.wait("tensor", *gm)
            rhs_t = xnT if which == 0 else yfull
            for kc in range(KC):
                last = kc == KC - 1
                ctx.op("tensor", "matmul", semname="TEN" if last else None,
                       out=sl.ap[:],
                       lhsT=wgo[:, buf * 2048 + kc * 128:
                                buf * 2048 + (kc + 1) * 128],
                       rhs=rhs_t[:, kc * T:(kc + 1) * T],
                       start=(kc == 0), stop=last)
            sl.wrote("TEN", ctx.count["TEN"])
            guards[("wgo", buf)] = ("TEN", ctx.count["TEN"])
            sl.pre_read(ctx, "scalar")
            if which == 0:
                ctx.wait("scalar", "CONST", consts_ld)
                ctx.op("scalar", "activation", semname="ACT",
                       out=gate_sb[:, oc * T:(oc + 1) * T], in_=sl.ap[:],
                       func=AF.Sigmoid, bias=bg_sb[:, oc:oc + 1])
            else:
                ctx.op("scalar", "copy", semname="ACT",
                       out=gate_sb[:, oc * T:(oc + 1) * T], in_=sl.ap[:])
            sl.read_done("ACT", ctx.count["ACT"])

        def emit_scan_group():
            g4 = st["group"]
            st["group"] += 1
            h0 = 4 * g4
            ctx.wait("vector", "DBB0", st["dbbc_mark"])
            for q in range(4):
                ctx.op("vector", "tensor_tensor",
                       out=d04[:, q * T:(q + 1) * T], in0=B_sb[:],
                       in1=dbbc4[:, q * T:(q + 1) * T], op=OP.add,
                       semname="VEC" if q == 3 else None)
            d0m = ctx.count["VEC"]
            if h0 + 4 < H:
                ctx.wait(q_scan, "VEC", d0m)
                for q in range(4):
                    nh = h0 + 4 + q
                    ctx.dma(
                        q_scan, dbbc4[:, q * T:(q + 1) * T],
                        db_dram.ap()[nh:nh + 1, :].partition_broadcast(128)
                        .squeeze(1), "DBB0")
                st["dbbc_mark"] = ctx.count["DBB0"]
            ctx.wait("scalar", "VEC", d0m)
            ctx.op("scalar", "activation", out=a14[:], in_=d04[:], func=AF.Abs)
            ctx.op("scalar", "activation", out=a14[:], in_=a14[:],
                   func=AF.Exp, scale=-1.0)
            ctx.op("scalar", "activation", out=a14[:], in_=a14[:],
                   func=AF.Ln, bias=1.0, semname="ACT")
            lnm = ctx.count["ACT"]
            ctx.op("vector", "tensor_scalar", out=dl4[:], in0=d04[:],
                   scalar1=0.0, scalar2=None, op0=OP.max)
            ctx.wait("vector", "ACT", lnm)
            ctx.op("vector", "tensor_tensor", out=dl4[:], in0=dl4[:],
                   in1=a14[:], op=OP.add, semname="VEC")
            dlm = ctx.count["VEC"]
            ctx.wait("scalar", "VEC", dlm)
            ctx.wait("scalar", "CONST", consts_ld)
            for q in range(4):
                ctx.op("scalar", "activation", out=d04[:, q * T:(q + 1) * T],
                       in_=dl4[:, q * T:(q + 1) * T], func=AF.Exp,
                       scale=a_sb[:, h0 + q:h0 + q + 1],
                       semname="ACT" if q == 3 else None)
            expm = ctx.count["ACT"]
            ctx.op("vector", "tensor_tensor", out=a14[:], in0=dl4[:],
                   in1=dl4[:], op=OP.mult)
            ctx.wait("vector", "ACT", expm)
            for q in range(4):
                h = h0 + q
                init = 0.0 if p == 0 else hcarry[:, h:h + 1]
                ctx.op("vector", "tensor_tensor_scan",
                       out=dl4[:, q * T:(q + 1) * T],
                       data0=d04[:, q * T:(q + 1) * T],
                       data1=a14[:, q * T:(q + 1) * T],
                       initial=init, op0=OP.mult, op1=OP.add)
                g = guards.get(("mh", q))
                if g is not None:
                    ctx.wait("vector", *g)
                ctx.op("vector", "tensor_tensor",
                       out=sc_mh4[:, q * T:(q + 1) * T],
                       in0=dl4[:, q * T:(q + 1) * T],
                       in1=C_sb[:], op=OP.mult,
                       semname="VEC" if q == 3 else None)
                ctx.op("vector", "tensor_copy", out=hcarry[:, h:h + 1],
                       in_=dl4[:, q * T + T - 1:q * T + T])
            st["mh_mark"] = ctx.count["VEC"]

        def emit_y_group():
            ctx.phase = "y%d" % st["yg"]
            g4 = st["yg"]
            st["yg"] += 1
            if g4 == 0:
                ctx.wait("vector", "OUT0", ctx.count["OUT0"])
                ctx.wait("vector", "OUT1", ctx.count["OUT1"])
            ctx.wait("tensor", "VEC", st["mh_mark"])
            for q in range(4):
                mh_ap = sc_mh4[:, q * T:(q + 1) * T]
                sl = slotA if q % 2 == 0 else slotB
                sl.pre_write(ctx)
                ctx.op("tensor", "matmul", semname="TEN",
                       out=sl.ap[0:32, :], lhsT=one_sb[:, 0:32], rhs=mh_ap,
                       start=True, stop=True)
                guards[("mh", q)] = ("TEN", ctx.count["TEN"])
                sl.wrote("TEN", ctx.count["TEN"])
                sl.pre_read(ctx, "scalar")
                ctx.wait("scalar", "YB", ctx.count["YB"])
                ctx.op("scalar", "copy", semname="ACT",
                       out=ysb[q % 2], in_=sl.ap[0:32, :])
                sl.read_done("ACT", ctx.count["ACT"])
                if q == 0:
                    g = guards.get("ybuild")
                    if g is not None:
                        ctx.wait(q_scan, *g)
                ctx.wait(q_scan, "ACT", ctx.count["ACT"])
                ctx.dma(q_scan, ybuild[32 * q:32 * (q + 1), :], ysb[q % 2],
                        "YB")
            ctx.wait("vector", "YB", ctx.count["YB"])
            for c in (2 * g4, 2 * g4 + 1):
                ctx.op("vector", "tensor_scalar", out=sc_t1[:],
                       in0=xnT[:, c * T:(c + 1) * T],
                       scalar1=dp_sb[:, c:c + 1], scalar2=None,
                       op0=OP.mult)
                ctx.op("vector", "tensor_tensor", out=sc_t2[:],
                       in0=sc_t1[:], in1=ybuild[:], op=OP.add)
                ctx.op("vector", "tensor_tensor", semname="VEC",
                       out=yfull[:, c * T:(c + 1) * T],
                       in0=sc_t2[:], in1=gate_sb[:, c * T:(c + 1) * T],
                       op=OP.mult)
            guards["ybuild"] = ("VEC", ctx.count["VEC"])
            if g4 == H // 4 - 1:
                ctx.op("vector", "nop", semname="HC")
                st["yf"] = ctx.count["VEC"]
                st["ymm_done"] = ctx.count["TEN"]

        def aux():
            s = st["slot"]
            st["slot"] += 1
            if st["wg"] < KC and s % 2 == 1:
                emit_wgo_group(0, st["wg"])
                st["wg"] += 1
            if s % 4 == 3 and st["yg"] < st["group"]:
                emit_y_group()
            if s % 4 == 3 and st["group"] < H // 4:
                emit_scan_group()
            if st["yg"] == H // 4 and st["wo"] < KC and s % 2 == 0:
                emit_wgo_group(1, st["wo"])
                st["wo"] += 1

        inter_marks = {}
        for ns in range(NCORES):
            down_guard = guards.get("downTEN")
            for J2 in range(8):
                u = ns * 8 + J2
                if u < 18:
                    buf = u % 2
                else:
                    buf = (2, 3, 0, 1)[(u - 18) % 4]
                pm = {}
                for (plist, roff, nm, fam, q_) in (
                        (gplist, 0, "gp", "GP", "sync"),
                        (uplist, 1024, "up", "UP", "sync")):
                    g = guards.get((nm, buf))
                    if g is not None:
                        ctx.wait(q_, *g)
                    if nm == "gp" and buf < 2:
                        g = guards.get(("xld", buf))
                        if g is not None:
                            ctx.wait(q_, *g)
                    if u in (18, 19):
                        ctx.wait(q_, "VEC", st["yf"])
                    ctx.wait(q_, "CC", 2 if J2 < 4 else 4)
                    blk = (J2 // 4) + (0 if roff == 0 else 2)
                    r0 = blk * (NCORES * 512) + ns * 512 + (J2 % 4) * 128
                    src = cc1_out.ap()[r0:r0 + 128, :]
                    dst = plist[buf]
                    sem = fam + str(buf)
                    ctx.dma(q_, dst, src, sem)
                    pm[nm] = (sem, ctx.count[sem])
                for par in range(2):
                    ci = J2 * 2 + par
                    ctx.phase = "u%d.%d.%d" % (ns, J2, par)
                    subslots = []
                    for (plist, sid, nm) in ((gplist, 0, "gp"),
                                             (uplist, 1, "up")):
                        sl = next_bank()
                        subslots.append(sl)
                        sl.pre_write(ctx)
                        ctx.wait("tensor", *pm[nm])
                        pan_ap = plist[buf]
                        for kc in range(KC):
                            last = kc == KC - 1
                            c0 = kc * 256 + par * 128
                            lhs = pan_ap[:, c0:c0 + 128]
                            ctx.op("tensor", "matmul",
                                   semname="TEN" if last else None,
                                   out=sl.ap[:], lhsT=lhs,
                                   rhs=xnT[:, kc * T:(kc + 1) * T],
                                   start=(kc == 0), stop=last)
                        sl.wrote("TEN", ctx.count["TEN"])
                    guards[("gp", buf)] = ("TEN", ctx.count["TEN"])
                    guards[("up", buf)] = ("TEN", ctx.count["TEN"])
                    slg, slu = subslots
                    slg.pre_read(ctx, "scalar")
                    slu.pre_read(ctx, "scalar")
                    g = guards.get("gelu")
                    if g is not None:
                        ctx.wait("scalar", *g)
                    ctx.op("scalar", "copy", out=gf1[:], in_=slg.ap[:])
                    ctx.op("scalar", "activation", out=gf2[:], in_=slg.ap[:],
                           func=AF.Square)
                    ctx.op("scalar", "activation", out=gf3[:], in_=slu.ap[:],
                           func=AF.Copy, scale=0.5, semname="ACT")
                    gm = ctx.count["ACT"]
                    slg.read_done("ACT", gm)
                    slu.read_done("ACT", gm)
                    ctx.wait("vector", "ACT", gm)
                    ctx.op("vector", "tensor_scalar", out=gf2[:], in0=gf2[:],
                           scalar1=GELU_C1, scalar2=GELU_C0,
                           op0=OP.mult, op1=OP.add)
                    ctx.op("vector", "tensor_tensor", out=gf2[:], in0=gf1[:],
                           in1=gf2[:], op=OP.mult, semname="VEC")
                    ctx.wait("scalar", "VEC", ctx.count["VEC"])
                    ctx.op("scalar", "activation", out=gf2[:], in_=gf2[:],
                           func=AF.Tanh, semname="ACT")
                    ctx.wait("vector", "ACT", ctx.count["ACT"])
                    ctx.op("vector", "tensor_scalar", out=gf2[:], in0=gf2[:],
                           scalar1=1.0, scalar2=None, op0=OP.add)
                    ctx.op("vector", "tensor_tensor", out=gf4[:], in0=gf1[:],
                           in1=gf3[:], op=OP.mult)
                    if down_guard is not None and J2 == 0 and par == 0:
                        ctx.wait("vector", *down_guard)
                    ctx.op("vector", "tensor_tensor", semname="VEC",
                           out=inter[:, ci * T:(ci + 1) * T],
                           in0=gf4[:], in1=gf2[:], op=OP.mult)
                    inter_marks[ci] = ctx.count["VEC"]
                    guards["gelu"] = ("VEC", ctx.count["VEC"])
                    aux()
                if ns == NCORES - 1 and J2 == 7:
                    guards["units_done"] = ("TEN", ctx.count["TEN"])
            wdlist = [wdp[:, 0:2048], wdp[:, 2048:4096], sc_mh4[:]]
            for ht in range(KC):
                buf = ht % 2 if ns == 0 else (ns * KC + ht) % 3
                ctx.phase = "dn%d.%d" % (ns, ht)
                g = guards.get(("wd", buf))
                if g is not None:
                    ctx.wait("gpsimd", *g)
                if buf == 2:
                    ctx.wait("gpsimd", "TEN", st["ymm_done"])
                ctx.wait("gpsimd", "CC",
                         6 if (p == 0 and ns == 0) else (5 if ht < 8 else 6))
                r0 = (ht // 8) * (NCORES * 512) + ns * 512 \
                    + ((ht // 2) % 4) * 128
                src = cc2_out.ap()[r0:r0 + 128,
                                  (ht % 2) * 2048:(ht % 2 + 1) * 2048]
                dst = wdlist[buf]
                sem = "WD%d" % buf
                ctx.dma("gpsimd", dst, src, sem)
                sl = next_bank()
                sl.pre_write(ctx)
                ctx.wait("tensor", sem, ctx.count[sem])
                for cn in range(KC):
                    last = cn == KC - 1
                    if ht == 0:
                        ctx.wait("tensor", "VEC", inter_marks[cn])
                    ctx.op("tensor", "matmul", semname="TEN" if last else None,
                           out=sl.ap[:],
                           lhsT=wdlist[buf][:, cn * 128:(cn + 1) * 128],
                           rhs=inter[:, cn * T:(cn + 1) * T],
                           start=(cn == 0), stop=last)
                sl.wrote("TEN", ctx.count["TEN"])
                guards[("wd", buf)] = ("TEN", ctx.count["TEN"])
                if ns == 0:
                    sl.pre_read(ctx, "scalar")
                    ctx.op("scalar", "copy", semname="ACT",
                           out=acc[:, ht * T:(ht + 1) * T], in_=sl.ap[:])
                    sl.read_done("ACT", ctx.count["ACT"])
                else:
                    sl.pre_read(ctx, "vector")
                    ctx.op("vector", "tensor_tensor", semname="VEC",
                           out=acc[:, ht * T:(ht + 1) * T],
                           in0=acc[:, ht * T:(ht + 1) * T], in1=sl.ap[:],
                           op=OP.add)
                    sl.read_done("VEC", ctx.count["VEC"])
                aux()
            guards["downTEN"] = ("TEN", ctx.count["TEN"])
        assert st["group"] == H // 4 and st["yg"] == H // 4 \
            and st["wg"] == KC and st["wo"] == KC, st

    def emit_combine(ctx, p):
        acc_v = ctx.count["VEC"]
        acc_a = ctx.count["ACT"]
        for k4 in range(KC // 4):
            c0 = k4 * 4 * T
            ctx.wait("scalar", "VEC", acc_v)
            ctx.wait("scalar", "TEN", ctx.count["TEN"])
            g = guards.get("sgf")
            if g is not None:
                ctx.wait("scalar", *g)
            ctx.op("scalar", "activation", semname="ACT", out=d04[:],
                   in_=acc[:, c0:c0 + 4 * T], func=AF.Sigmoid)
            ctx.wait("vector", "ACT", ctx.count["ACT"])
            ctx.op("vector", "tensor_tensor", out=a14[:],
                   in0=acc[:, c0:c0 + 4 * T], in1=d04[:], op=OP.mult)
            ctx.op("vector", "tensor_tensor", semname="VEC",
                   out=gate_sb[:, c0:c0 + 4 * T],
                   in0=a14[:], in1=gate_sb[:, c0:c0 + 4 * T],
                   op=OP.mult)
            guards["sgf"] = ("VEC", ctx.count["VEC"])
        sg_done = ctx.count["VEC"]
        ctx.wait("tensor", "VEC", sg_done)
        xld_c = [dl4[:], dbbc4[:]]
        for tt in range(NT):
            buf = tt % 2
            row0 = p * T + tt * 128
            g = guards.get(("xldc", buf))
            if g is not None:
                ctx.wait("sync", *g)
            g = guards.get("units_done")
            if g is not None:
                ctx.wait("sync", *g)
            ctx.dma("sync", xld_c[buf], x_in[row0:row0 + 128, :],
                    "XLD%d" % buf)
            g = guards.get(("ostage", buf))
            if g is not None:
                ctx.wait("vector", *g)
            for ht in range(KC):
                sl = slotA if ht % 2 == 0 else slotB
                ctx.phase = "comb%d.%d" % (tt, ht)
                off = ht // 2
                sl.pre_write(ctx)
                ctx.op("tensor", "matmul", semname="TEN",
                       out=sl.b16[:, off * 128:(off + 1) * 128],
                       lhsT=gate_sb[:, ht * T + tt * 128: ht * T + (tt + 1) * 128],
                       rhs=idb_sb[:], is_transpose=True, start=True, stop=True)
                sl.wrote("TEN", ctx.count["TEN"])
                sl.pre_read(ctx, "vector")
                ctx.wait("vector", "XLD%d" % buf, ctx.count["XLD%d" % buf])
                ctx.op("vector", "tensor_tensor", semname="VEC",
                       out=out_stage[buf][:, ht * 128:(ht + 1) * 128],
                       in0=xld_c[buf][:, ht * 128:(ht + 1) * 128],
                       in1=sl.b16[:, off * 128:(off + 1) * 128], op=OP.add)
                sl.read_done("VEC", ctx.count["VEC"])
            vm = ctx.count["VEC"]
            ctx.wait("sync", "VEC", vm)
            sem = "OUT%d" % buf
            ctx.dma("sync", out_t[row0:row0 + 128, :], out_stage[buf], sem)
            guards[("xldc", buf)] = (sem, ctx.count[sem])
            guards[("ostage", buf)] = (sem, ctx.count[sem])

    def schedule(ctx):
        for sl in slots + [slotA, slotB]:
            sl.reset()
        guards.clear()
        ffn_rot[0] = 0
        consts_ld, consts_gld = emit_consts(ctx)
        emit_dequant(ctx, consts_ld, consts_gld)
        for p in range(NPASS):
            emit_stats_xn(ctx, p, consts_ld)
            emit_merged(ctx, p, consts_ld, consts_gld)
            emit_combine(ctx, p)
        ctx.wait("sync", "OUT0", ctx.count["OUT0"])
        ctx.wait("sync", "OUT1", ctx.count["OUT1"])

    with nc.Block() as block:
        @block.sync
        def _(eng):
            ctx.begin("sync", eng)
            schedule(ctx)

        @block.tensor
        def _(eng):
            ctx.begin("tensor", eng)
            schedule(ctx)

        @block.vector
        def _(eng):
            ctx.begin("vector", eng)
            schedule(ctx)

        @block.scalar
        def _(eng):
            ctx.begin("scalar", eng)
            schedule(ctx)

        @block.gpsimd
        def _(eng):
            ctx.begin("gpsimd", eng)
            schedule(ctx)

    st.close()
    return nc


def prep_inputs(inputs):
    perm = build_perm()
    x = np.asarray(inputs["L0_attn_x"], dtype=np.float32)
    Wx = np.asarray(inputs["L0_attn_Wx"], dtype=np.float32)
    Wg = np.asarray(inputs["L0_attn_Wg"], dtype=np.float32)
    bg = np.asarray(inputs["L0_attn_bg"], dtype=np.float32)
    Wo = np.asarray(inputs["L0_attn_Wo"], dtype=np.float32)
    A = np.asarray(inputs["L0_attn_A"], dtype=np.float32)
    Dp = np.asarray(inputs["L0_attn_Dp"], dtype=np.float32)
    gpkf = np.asarray(inputs["L0_ffn_gate_W_packed"], dtype=np.int32)
    gscales = np.asarray(inputs["L0_ffn_gate_scales"], dtype=np.float32)
    upkf = np.asarray(inputs["L0_ffn_up_W_packed"], dtype=np.int32)
    uscales = np.asarray(inputs["L0_ffn_up_scales"], dtype=np.float32)
    dpkf = np.asarray(inputs["L0_ffn_down_W_packed"], dtype=np.int32)
    dscales = np.asarray(inputs["L0_ffn_down_scales"], dtype=np.float32)

    wx_p = np.ascontiguousarray(Wx[perm])
    wg_p = np.ascontiguousarray(Wg[perm][:, perm])
    wo_p = np.ascontiguousarray(Wo[perm])
    wgf = np.ascontiguousarray(
        wg_p.reshape(KC, 128, KC, 128).transpose(2, 1, 0, 3).reshape(DM, DM))
    wof = np.ascontiguousarray(
        wo_p.reshape(KC, 128, KC, 128).transpose(2, 1, 0, 3).reshape(DM, DM))
    bg_in = np.ascontiguousarray(bg[perm].reshape(KC, 128).T)
    dp_in = np.ascontiguousarray(np.repeat(Dp, HD)[perm].reshape(KC, 128).T)
    a_in = np.ascontiguousarray(A.T)
    mlp_perm = np.arange(SLICE_N).reshape(4, 2, 128, 2) \
        .transpose(0, 1, 3, 2).reshape(-1)
    idn_f = np.eye(128, dtype=np.float32)
    idn_b = np.eye(128, dtype=np.float32).astype(ml_dtypes.bfloat16)
    ones_b = np.ones((128, 32), dtype=np.float32).astype(ml_dtypes.bfloat16)
    c22_in = np.full((128, 1), 22, dtype=np.int32)
    dscT = np.ascontiguousarray(dscales.T)

    in_maps = []
    for i in range(NCORES):
        in_maps.append({
            "x_in": np.ascontiguousarray(x[i]),
            "wx_p": wx_p,
            "bg_in": bg_in, "dp_in": dp_in, "a_in": a_in,
            "gpk": np.ascontiguousarray(gpkf[i * SLICE_N:(i + 1) * SLICE_N]),
            "upk": np.ascontiguousarray(upkf[i * SLICE_N:(i + 1) * SLICE_N]),
            "dpk": np.ascontiguousarray(dpkf[:, i * SLICE_J:(i + 1) * SLICE_J]),
            "gsc": np.ascontiguousarray(
                gscales[i * SLICE_N:(i + 1) * SLICE_N].T[:, mlp_perm]),
            "usc": np.ascontiguousarray(
                uscales[i * SLICE_N:(i + 1) * SLICE_N].T[:, mlp_perm]),
            "dsc": np.ascontiguousarray(dscT[i * 64:(i + 1) * 64]),
            "idn_f": idn_f, "idn_b": idn_b, "ones_b": ones_b, "c22_in": c22_in,
            "wgf": wgf, "wof": wof,
        })
    return in_maps


def kernel(**inputs):
    in_maps = prep_inputs(inputs)
    nc = build()
    res = run_bass_kernel_spmd(nc, in_maps, core_ids=list(range(NCORES)))
    out = np.stack([np.asarray(res.results[i]["out"]) for i in range(NCORES)],
                   axis=0)
    return out.astype(np.float32)
